# version 17
# baseline (speedup 1.0000x reference)
import numpy as np
from collections import defaultdict, deque
from contextlib import ExitStack

import concourse.bass as bass
import concourse.mybir as mybir
from concourse.bass_utils import run_bass_kernel_spmd

F32 = mybir.dt.float32
BF16 = mybir.dt.bfloat16
I32 = mybir.dt.int32
AF = mybir.ActivationFunctionType
ALU = mybir.AluOpType

D = 2048
T = 4096
TOWN = 2048
H = 16
E = 128
HF = 8192
R = 16
NDC = D // 128
NHC = HF // 128
ISCALE = float(1.0 / np.sqrt(np.float32(128.0)))
LNEPS = 1e-5

ENGS = ("pe", "act", "dve", "sp", "gp")
STREAM = {"pe": "pe", "act": "act", "ald": "act", "dve": "dve",
          "sp": "sp", "gp": "gp", "po": "gp"}


class Prog:
    def __init__(self):
        self.ops = []
        self.cnt = defaultdict(int)

    def emit(self, stream, fn, waits=(), incs=(), tag=""):
        w = {}
        for s, v in waits:
            if v > 0:
                w[s] = max(w.get(s, 0), v)
        ws = sorted(w.items())
        self.ops.append((STREAM[stream], fn, ws, list(incs), tag))
        out = []
        for s, n in incs:
            self.cnt[s] += n
            out.append((s, self.cnt[s]))
        return out

    def check(self):
        qs = {e: deque() for e in ENGS}
        for opr in self.ops:
            qs[opr[0]].append(opr)
        cnt = defaultdict(int)
        total = len(self.ops)
        done = 0
        while done < total:
            prog = False
            for e in ENGS:
                while qs[e]:
                    _, fn, ws, incs, tag = qs[e][0]
                    if all(cnt[s] >= v for s, v in ws):
                        for s, n in incs:
                            cnt[s] += n
                        qs[e].popleft()
                        done += 1
                        prog = True
                    else:
                        break
            if not prog:
                blocked = {e: (qs[e][0][4], [(s, v, cnt[s]) for s, v in qs[e][0][2]
                                             if cnt[s] < v])
                           for e in ENGS if qs[e]}
                raise RuntimeError(f"schedule deadlock: {blocked}")


class Buf:
    __slots__ = ("ready", "readers")

    def __init__(self):
        self.ready = {}
        self.readers = {}

    @staticmethod
    def _merge(d, ms):
        for s, v in ms:
            d[s] = max(d.get(s, 0), v)

    def wr_waits(self):
        return list(self.readers.items()) + list(self.ready.items())

    def rd_waits(self):
        return list(self.ready.items())

    def set_written(self, ms):
        self.ready = dict(ms)
        self.readers = {}

    def add_reader(self, ms):
        self._merge(self.readers, ms)


def build_program(dbg=False):
    nc = bass.Bass("TRN2", target_bir_lowering=False, debug=False)
    dri = lambda name, shape, dt=F32: nc.dram_tensor(name, shape, dt, kind="ExternalInput").ap()

    xfm = dri("xfm", [D, T])
    xown = dri("xown", [D, TOWN])
    wq = dri("wq", [D, H * E])
    wk = dri("wk", [D, H * E])
    wv = dri("wv", [D, H * E])
    wo = dri("wo", [H * E, D])
    bq = dri("bq", [E, H])
    bk = dri("bk", [E, H])
    bv = dri("bv", [1, H * E])
    bo = dri("bo", [128, NDC])
    msk = dri("msk", [128, 256])
    qupt = dri("qupt", [D, HF], I32)
    supt = dri("supt", [32, HF])
    qdnt = dri("qdnt", [HF, D], I32)
    sdnt = dri("sdnt", [128, D])
    sel2 = dri("sel2", [2, 128])
    idn = dri("idn", [128, 128])
    bup = dri("bup", [128, NHC])
    bdn = dri("bdn", [128, NDC])
    aupw = dri("aupw", [D, R])
    buplw = dri("buplw", [R, HF])
    adnw = dri("adnw", [HF, R])
    bdnlw = dri("bdnlw", [R, D])

    out = nc.dram_tensor("out", [D, TOWN], F32, kind="ExternalOutput").ap()

    dbg_out = {}
    if dbg:
        for nm, shape, dt in [("dq0o", [E, TOWN], BF16), ("dk0", [E, T], BF16),
                              ("dv0", [T, E], BF16), ("dz", [128, 32768], BF16),
                              ("dr", [128, 32768], BF16), ("dxl", [128, 32768], BF16),
                              ("dx2", [128, 32768], BF16), ("dup", [HF, 128], BF16),
                              ("dgup", [128, NDC, 128], BF16), ("dl", [128, 512], F32)]:
            dbg_out[nm] = nc.dram_tensor(nm, shape, dt, kind="ExternalOutput").ap()
    q_scr = nc.dram_tensor("q_scr", [H, E, TOWN], BF16, kind="Internal").ap()
    k_scr = nc.dram_tensor("k_scr", [H, E, T], BF16, kind="Internal").ap()
    v_scr = nc.dram_tensor("v_scr", [H, T, E], BF16, kind="Internal").ap()
    gup = nc.dram_tensor("gup", [NHC, 128, NDC, 128], BF16, kind="Internal").ap()
    gdn = nc.dram_tensor("gdn", [NDC, 4, 128, 16, 128], BF16, kind="Internal").ap()

    p = Prog()
    B = defaultdict(Buf)

    with ExitStack() as ctx:
        en = ctx.enter_context
        arena = en(nc.sbuf_tensor("arena", [128, 131072], mybir.dt.uint8))
        a_bf = arena[:].bitcast(BF16)
        xh_bf = a_bf[:, 0:32768]
        zall = a_bf[:, 0:32768]
        xlnb = a_bf[:, 0:32768]
        rbig = a_bf[:, 32768:65536]
        x2b = a_bf[:, 32768:65536]
        p_sb = a_bf[:, 32768:36864]
        kq0 = 36864
        kv_bf = a_bf[:, kq0:kq0 + 16384]
        q0 = kq0 + 16384
        q_bf2 = a_bf[:, q0:q0 + 4096]
        xh_up = a_bf[:, 32768:65536]

        wsb = en(nc.sbuf_tensor("wsb", [128, 8192], BF16))
        stg = en(nc.sbuf_tensor("stg", [128, 3072], F32))
        r2b = en(nc.sbuf_tensor("r2b", [128, 8192], BF16))
        r2f = r2b[:].bitcast(F32)
        dq_in_s = [r2f[:, 2048:3072].bitcast(BF16)[:, 0:2048],
                   r2f[:, 3072:4096].bitcast(BF16)[:, 0:2048]]
        dq_out_s = [stg[:, 0:1024].bitcast(BF16)[:, 0:2048],
                    stg[:, 1024:2048].bitcast(BF16)[:, 0:2048]]
        mskb = en(nc.sbuf_tensor("mskb", [128, 256], BF16))

        onesm = en(nc.sbuf_tensor("onesm", [128, 128], BF16))
        onesr = en(nc.sbuf_tensor("onesr", [1, 128], BF16))
        bq_sb = en(nc.sbuf_tensor("bq_sb", [128, H], F32))
        bk_sb = en(nc.sbuf_tensor("bk_sb", [128, H], F32))
        bv_sb = en(nc.sbuf_tensor("bv_sb", [1, H * E], BF16))
        bo_sb = en(nc.sbuf_tensor("bo_sb", [128, NDC], F32))
        bup_sb = en(nc.sbuf_tensor("bup_sb", [128, NHC], F32))
        bdn_sb = en(nc.sbuf_tensor("bdn_sb", [128, NDC], F32))
        aup_sb = en(nc.sbuf_tensor("aup_sb", [128, NDC * R], BF16))
        adn_sb = en(nc.sbuf_tensor("adn_sb", [128, NHC * R], BF16))
        bupl_sb = en(nc.sbuf_tensor("bupl_sb", [R, 2048], BF16))
        bdnl_sb = en(nc.sbuf_tensor("bdnl_sb", [R, D], BF16))
        tup_sb = en(nc.sbuf_tensor("tup_sb", [R, 512], BF16))
        t2_sb = en(nc.sbuf_tensor("t2_sb", [R, 512], BF16))
        srow_sb = en(nc.sbuf_tensor("srow_sb", [2, 4096], BF16))
        sel_sb = en(nc.sbuf_tensor("sel_sb", [2, 128], BF16))
        idn_sb = en(nc.sbuf_tensor("idn_sb", [128, 128], BF16))
        epsb = en(nc.sbuf_tensor("epsb", [128, 1], F32))

        psAB = en(nc.psum_tensor("psAB", [128, 2048], F32))
        psCD = en(nc.psum_tensor("psCD", [128, 1024], F32))
        psEF = en(nc.psum_tensor("psEF", [128, 1024], F32))

        def bank(i, n=1):
            if i + n <= 4:
                return psAB[:, i * 512:(i + n) * 512]
            if 4 <= i and i + n <= 6:
                return psCD[:, (i - 4) * 512:(i - 4 + n) * 512]
            assert 6 <= i and i + n <= 8
            return psEF[:, (i - 6) * 512:(i - 6 + n) * 512]

        sems = {}
        for s in ("cst", "cstg", "xh0", "xh1", "wsl0", "wsl1", "wsl2", "wsl3",
                  "qw0", "qw1", "qw2", "qw3", "dqi0", "dqi1", "srw0", "srw1", "gw", "xrs",
                  "kv0", "kv1", "vv0", "vv1", "qq0", "qq1", "bl0", "bl1",
                  "st", "pe", "act", "dve", "po"):
            sems[s] = en(nc.semaphore(s))

        def op(stream, sem, fn, wbufs=(), rbufs=(), waits=(), tag="", inc=1):
            ws = list(waits)
            for wb in wbufs:
                ws += B[wb].wr_waits()
            for rb in rbufs:
                ws += B[rb].rd_waits()
            ms = p.emit(stream, fn, ws, [(sem, inc)], tag)
            for wb in wbufs:
                B[wb].set_written(ms)
            for rb in rbufs:
                B[rb].add_reader(ms)
            return ms

        def dma(stream, sem, dst, src, wbuf=None, rbufs=(), waits=(), tag="", tr=False):
            fn = (lambda e, d=dst, s=src: e.dma_start(out=d, in_=s, transpose=True)) if tr \
                else (lambda e, d=dst, s=src: e.dma_start(out=d, in_=s))
            return op(stream, sem, fn, wbufs=([wbuf] if wbuf else ()), rbufs=rbufs,
                      waits=waits, tag=tag, inc=16)

        def mm(dst, lhsT, rhs, start, stop, wbufs=(), rbufs=(), waits=(), tag=""):
            return op("pe", "pe",
                      lambda e, d=dst, l=lhsT, r=rhs, st=start, sp=stop:
                      e.matmul(d, l, r, start=st, stop=sp),
                      wbufs, rbufs, waits, tag)

        def batch_seal(sem, bufs):
            tot = p.cnt[sem]
            for b_ in bufs:
                B[b_].ready = {sem: tot}
        wslot = lambda k: wsb[:, k * 2048:(k + 1) * 2048]
        def qsf(k):
            if k < 2:
                return stg[:, 2048 + k * 512:2048 + (k + 1) * 512]
            return r2f[:, 1024 + (k - 2) * 512:1024 + (k - 1) * 512]
        qsb = lambda k: qsf(k).bitcast(BF16)[:, 0:512]

        op("dve", "dve", lambda e: e.memset(onesm[:], 1.0), wbufs=["onesm"], tag="ones")
        op("dve", "dve", lambda e: e.memset(onesr[:], 1.0), wbufs=["onesr"], tag="onesr")
        op("dve", "dve", lambda e: e.memset(epsb[:], LNEPS), wbufs=["epsb"], tag="eps")
        dma("sp", "cst", bq_sb[:], bq, wbuf="bq", tag="bq")
        dma("sp", "cst", bk_sb[:], bk, wbuf="bk", tag="bk")
        dma("sp", "cst", bo_sb[:], bo, wbuf="bo", tag="bo")
        dma("sp", "cst", bup_sb[:], bup, wbuf="bup", tag="bup")
        dma("sp", "cst", bdn_sb[:], bdn, wbuf="bdn", tag="bdn")
        dma("gp", "cstg", bv_sb[:], bv, wbuf="bv", tag="bv")
        dma("gp", "cstg", mskb[:], msk, wbuf="msk", tag="msk")
        dma("gp", "cstg", aup_sb[:].rearrange("p (c r) -> p c r", c=NDC),
            aupw.rearrange("(c p) r -> p c r", p=128), wbuf="aup", tag="aup")
        dma("gp", "cstg", adn_sb[:].rearrange("p (c r) -> p c r", c=NHC),
            adnw.rearrange("(c p) r -> p c r", p=128), wbuf="adn", tag="adn")
        dma("gp", "cstg", bdnl_sb[:], bdnlw, wbuf="bdnl", tag="bdnl")
        batch_seal("cst", ["bq", "bk", "bo", "bup", "bdn"])
        batch_seal("cstg", ["bv", "msk", "aup", "adn", "bdnl"])

        def load_whead(src, h, slot, tag):
            sr = src.rearrange("(c p) n -> p c n", p=128)[:, :, h * E:(h + 1) * E]
            return dma("gp", f"wsl{slot}", wslot(slot).rearrange("p (c e) -> p c e", c=NDC),
                       sr, wbuf=f"wsb{slot}", tag=tag)

        dma("gp", "cstg", sel_sb[:], sel2, wbuf="sel", tag="sel2")
        dma("gp", "cstg", idn_sb[:], idn, wbuf="idn", tag="idn")
        dq_jobs = [("u", dc, hq) for dc in range(NDC) for hq in range(4)] + \
                  [("d", hc, 0) for hc in range(NHC)]
        dq_state = {"lc": 0, "cc": 0}

        def _dq_aps(i):
            kind, a1, a2 = dq_jobs[i]
            if kind == "u":
                dc, hq = a1, a2
                return (qupt[dc * 128:(dc + 1) * 128, hq * 2048:(hq + 1) * 2048],
                        supt[2 * dc:2 * dc + 2, hq * 2048:(hq + 1) * 2048],
                        gup[hq * 16:(hq + 1) * 16, :, dc, :], f"u{dc}.{hq}")
            hc = a1
            return (qdnt[hc * 128:(hc + 1) * 128, :],
                    sdnt[2 * hc:2 * hc + 2, :],
                    gdn[:, hc // 16, :, hc % 16, :], f"d{hc}")

        def _dq_load(i):
            s = i % 2
            qa, sa, _, tg = _dq_aps(i)
            dma("gp", f"dqi{s}", dq_in_s[s], qa, wbuf=f"dqin{s}", tag=f"{tg}i")
            dma("gp", f"srw{s}", srow_sb[:, s * 2048:(s + 1) * 2048], sa,
                wbuf=f"srow{s}", tag=f"{tg}s")

        def _dq_compute(i):
            s = i % 2
            _, _, ga, tg = _dq_aps(i)
            for pc in range(4):
                bk_i = 4 + pc
                mm(bank(bk_i), sel_sb[:],
                   srow_sb[:, s * 2048 + pc * 512:s * 2048 + (pc + 1) * 512],
                   start=True, stop=True,
                   wbufs=[f"ps{bk_i}"],
                   rbufs=(["sel", f"srow{s}"] if pc in (0, 3) else [f"srow{s}"]),
                   tag=f"{tg}.b{pc}")
                op("dve", "dve",
                   lambda e, o=dq_out_s[s][:, pc * 512:(pc + 1) * 512],
                          ii=dq_in_s[s][:, pc * 512:(pc + 1) * 512],
                          sr=bank(bk_i):
                   e.scalar_tensor_tensor(out=o, in0=ii, scalar=-7.5, in1=sr,
                                          op0=ALU.add, op1=ALU.mult),
                   wbufs=([f"dqout{s}"] if pc in (0, 3) else ()),
                   rbufs=[f"dqin{s}" if pc == 0 else f"dqout{s}", f"ps{bk_i}"],
                   tag=f"{tg}.q{pc}")
            dma("gp", "gw", ga.rearrange("k p e -> p k e"),
                dq_out_s[s].rearrange("p (k e) -> p k e", k=16),
                rbufs=[f"dqout{s}"], tag=f"{tg}w")

        def run_dq(n):
            for _ in range(n):
                if dq_state["lc"] < min(dq_state["cc"] + 2, len(dq_jobs)):
                    _dq_load(dq_state["lc"])
                    dq_state["lc"] += 1
                if dq_state["cc"] < dq_state["lc"]:
                    _dq_compute(dq_state["cc"])
                    dq_state["cc"] += 1

        load_whead(wq, 0, 0, "wq0pre")
        load_whead(wq, 1, 1, "wq1pre")
        for c in range(NDC):
            dma("gp", f"xh{c % 2}", xh_bf[:, c * 2048:(c + 1) * 2048],
                xown[c * 128:(c + 1) * 128, :], wbuf=f"xq{c}",
                waits=[(f"xh{c % 2}", 16 * (c // 2))], tag=f"xo{c}")

        def xup_load(c):
            dma("gp", f"xh{c % 2}", xh_up[:, c * 2048:(c + 1) * 2048],
                xfm[c * 128:(c + 1) * 128, 0:2048], wbuf=f"xa{c}",
                waits=[(f"xh{c % 2}", 16 * (8 + c // 2))], tag=f"xfa{c}")

        for h in range(H):
            slot = h % 4
            if h + 2 < H:
                load_whead(wq, h + 2, (h + 2) % 4, f"wq{h + 2}")
            for qb in range(4):
                bi = (h * 4 + qb) % 4
                dst = bank(bi)
                for c in range(NDC):
                    rb = []
                    if h == 0 and qb == 0:
                        rb.append(f"xq{c}")
                    if h == H - 1 and qb == 3:
                        rb.append(f"xq{c}")
                    if c in (0, NDC - 1):
                        rb.append(f"wsb{slot}")
                    mm(dst, wslot(slot)[:, c * E:(c + 1) * E],
                       xh_bf[:, c * 2048 + qb * 512:c * 2048 + (qb + 1) * 512],
                       start=(c == 0), stop=(c == NDC - 1),
                       wbufs=([f"ps{bi}"] if c in (0, NDC - 1) else ()), rbufs=rb,
                       tag=f"q{h}.{qb}.{c}")
                sl = (h * 4 + qb) % 4
                op("act", "act",
                   lambda e, d=qsb(sl), s=dst, b=bq_sb[:, h:h + 1]:
                   e.activation(out=d, in_=s, func=AF.Identity, bias=b, scale=1.0),
                   wbufs=[f"qs{sl}"], rbufs=[f"ps{bi}", "bq"], tag=f"qe{h}.{qb}")
                dma("sp", f"qw{sl}", q_scr[h, :, qb * 512:(qb + 1) * 512], qsb(sl),
                    rbufs=[f"qs{sl}"], tag=f"qw{h}.{qb}")
            run_dq(2)
            xup_load(h)

        def xlo_load(c):
            dma("gp", f"xh{c % 2}", xh_bf[:, c * 2048:(c + 1) * 2048],
                xfm[c * 128:(c + 1) * 128, 2048:4096], wbuf=f"xq{c}",
                waits=[(f"xh{c % 2}", 16 * (16 + c // 2))], tag=f"xfb{c}")

        for half in range(2):
            t0 = half * 2048
            xh = xh_up if half == 0 else xh_bf
            xk = (lambda c: f"xa{c}") if half == 0 else (lambda c: f"xq{c}")
            load_whead(wk, 0, 0, f"wk{half}.0pre")
            load_whead(wk, 1, 1, f"wk{half}.1pre")
            for h in range(H):
                slot = h % 4
                run_dq(2)
                if half == 0:
                    xlo_load(h)
                if h + 2 < H:
                    load_whead(wk, h + 2, (h + 2) % 4, f"wk{half}.{h + 2}")
                for tb in range(4):
                    bi = tb % 4
                    dst = bank(bi)
                    for c in range(NDC):
                        rb = []
                        if h == 0 and tb == 0:
                            rb.append(xk(c))
                        if c in (0, NDC - 1):
                            rb.append(f"wsb{slot}")
                        mm(dst, wslot(slot)[:, c * E:(c + 1) * E],
                           xh[:, c * 2048 + tb * 512:c * 2048 + (tb + 1) * 512],
                           start=(c == 0), stop=(c == NDC - 1),
                           wbufs=([f"ps{bi}"] if c in (0, NDC - 1) else ()), rbufs=rb,
                           tag=f"k{half}.{h}.{tb}.{c}")
                    sl = (h * 4 + tb) % 4
                    op("act", "act",
                       lambda e, d=qsb(sl), s=dst, b=bk_sb[:, h:h + 1]:
                       e.activation(out=d, in_=s, func=AF.Identity, bias=b, scale=1.0),
                       wbufs=[f"qs{sl}"], rbufs=[f"ps{bi}", "bk"], tag=f"ke{half}.{h}.{tb}")
                    dma("sp", f"qw{sl}", k_scr[h, :, t0 + tb * 512:t0 + (tb + 1) * 512],
                        qsb(sl), rbufs=[f"qs{sl}"], tag=f"kw{half}.{h}.{tb}")
            for g in range(4):
                run_dq(4)
                sr = wv.rearrange("(c p) n -> p c n", p=128)[:, :, g * 512:(g + 1) * 512]
                slab_ms = op("gp", "wsl0",
                             lambda e, d=wsb[:, 0:8192].rearrange("p (c e) -> p c e", c=NDC), s=sr:
                             e.dma_start(out=d, in_=s),
                             wbufs=["wsb0", "wsb1", "wsb2", "wsb3"],
                             tag=f"wv{half}.{g}", inc=16)
                for kt in range(16):
                    bi = kt % 4
                    dst = bank(bi)
                    for c in range(NDC):
                        rb = []
                        if g == 3 and kt == 15:
                            rb.append(xk(c))
                        if kt in (0, 15):
                            rb.append(f"wsb{c // 4}")
                        mm(dst, xh[:, c * 2048 + kt * 128:c * 2048 + (kt + 1) * 128],
                           wsb[:, c * 512:(c + 1) * 512],
                           start=(c == 0), stop=False,
                           wbufs=([f"ps{bi}"] if c == 0 else ()), rbufs=rb,
                           tag=f"v{half}.{g}.{kt}.{c}")
                    mm(dst, onesr[:], bv_sb[:, g * 512:(g + 1) * 512],
                       start=False, stop=True, wbufs=[f"ps{bi}"],
                       rbufs=["onesr", "bv"], tag=f"vb{half}.{g}.{kt}")
                    sl = kt % 4
                    op("dve", "dve",
                       lambda e, d=qsb(sl), s=dst: e.tensor_copy(out=d, in_=s),
                       wbufs=[f"qs{sl}"], rbufs=[f"ps{bi}"], tag=f"ve{half}.{g}.{kt}")
                    for hh in range(4):
                        dma("sp", f"qw{sl}",
                            v_scr[g * 4 + hh, t0 + kt * 128:t0 + (kt + 1) * 128, :],
                            qsb(sl)[:, hh * 128:(hh + 1) * 128],
                            rbufs=[f"qs{sl}"], tag=f"vw{half}.{g}.{kt}.{hh}")

        run_dq(200)
        deq_done = [("gw", p.cnt["gw"]), ("dqi0", p.cnt["dqi0"]), ("dqi1", p.cnt["dqi1"])]
        grids_done = [("gw", p.cnt["gw"])]

        k_half = lambda bb: kv_bf[:, bb * 4096:(bb + 1) * 4096]
        v_half = lambda bb: kv_bf[:, 8192 + bb * 4096:8192 + (bb + 1) * 4096]
        q_slot = lambda bb: q_bf2[:, bb * 2048:(bb + 1) * 2048]
        pslot = lambda s: p_sb[:, s * 1024:(s + 1) * 1024]
        lden = lambda cid: r2f[:, cid * 256:(cid + 1) * 256]
        linv = lambda cid: r2f[:, 512 + cid * 256:512 + (cid + 1) * 256]

        p1_done = [(f"qw{i}", p.cnt[f"qw{i}"]) for i in range(4)]
        sctr = [0]
        hstate = {}

        def emit_quad_S(h, kb, c, cid, qq, nkt):
            ss = sctr[0] % 3
            sctr[0] += 1
            sdst = bank(2 * ss, 2)
            spair = [f"ps{2 * ss}", f"ps{2 * ss + 1}"]
            for t_ in range(4):
                j = 4 * qq + t_
                li = max(0, j // 2 - 2 * c)
                n = 256 - 128 * li
                edge = (j >= 4 * c)
                rb = []
                if t_ == 3 or not hstate.get("s_seen"):
                    rb = [f"kbuf{kb}", f"qbuf{kb}"]
                    hstate["s_seen"] = True
                if edge:
                    mm(sdst[:, t_ * 256 + li * 128:t_ * 256 + li * 128 + 128],
                       idn_sb[:], mskb[:, (j & 1) * 128:((j & 1) + 1) * 128],
                       start=True, stop=False,
                       wbufs=(spair if t_ == 0 else ()),
                       rbufs=["idn", "msk"], tag=f"MB{h}.{c}.{j}")
                mm(sdst[:, t_ * 256 + li * 128:(t_ + 1) * 256],
                   k_half(kb)[:, j * 128:(j + 1) * 128],
                   q_slot(kb)[:, c * 256 + li * 128:(c + 1) * 256],
                   start=not edge, stop=True,
                   wbufs=(spair if (t_ == 3 or (t_ == 0 and not edge)) else ()),
                   rbufs=rb, tag=f"S{h}.{c}.{j}")
            op("act", "act",
               lambda e, o=pslot(ss).rearrange("p (a b) -> p a b", b=256),
                      s=sdst.rearrange("p (a b) -> p a b", b=256):
               e.activation(out=o, in_=s, func=AF.Exp, scale=ISCALE),
               wbufs=[f"pp{ss}"], rbufs=spair, tag=f"E{h}.{c}.{qq}")
            return ss

        def emit_quad_AV(h, kb, c, cid, qq, nkt, ss):
            zlb = 6 + cid
            zdst = bank(zlb)[:, 0:256]
            ldst = bank(zlb)[:, 256:512]
            last_q = (qq == nkt // 4 - 1)
            for t_ in range(4):
                j = 4 * qq + t_
                li = max(0, j // 2 - 2 * c)
                n = 256 - 128 * li
                rb = [f"pp{ss}"]
                if t_ == 3 or not hstate.get("v_seen"):
                    rb.append(f"vbuf{kb}")
                    hstate["v_seen"] = True
                if c == 7 and j == nkt - 1:
                    rb += [f"kbuf{kb}", f"qbuf{kb}", f"vbuf{kb}"]
                mm(zdst[:, li * 128:256],
                   v_half(kb)[:, j * 128:(j + 1) * 128],
                   pslot(ss)[:, t_ * 256 + li * 128:(t_ + 1) * 256],
                   start=(j == 0), stop=False,
                   wbufs=([f"ps{zlb}"] if j == 0 else ()),
                   rbufs=rb, tag=f"A{h}.{c}.{j}")
                mm(ldst[:, li * 128:256], onesm[:],
                   pslot(ss)[:, t_ * 256 + li * 128:(t_ + 1) * 256],
                   start=False, stop=(last_q and t_ == 3),
                   wbufs=([f"ps{zlb}"] if (last_q and t_ == 3) else ()),
                   rbufs=(["onesm", f"pp{ss}"] if t_ == 3 else [f"pp{ss}"]),
                   tag=f"L{h}.{c}.{j}")

        def emit_epi(h, c, cid):
            zlb = 6 + cid
            op("dve", "dve", lambda e, d=lden(cid), s=bank(zlb)[:, 256:512]:
               e.tensor_copy(out=d, in_=s),
               wbufs=[f"ldn{cid}"], rbufs=[f"ps{zlb}"], tag=f"le{h}.{c}")
            op("dve", "dve", lambda e, d=linv(cid), s=lden(cid): e.reciprocal(out=d, in_=s),
               wbufs=[f"lin{cid}"], rbufs=[f"ldn{cid}"], tag=f"lr{h}.{c}")
            op("dve", "dve",
               lambda e, d=zall[:, h * 2048 + c * 256:h * 2048 + (c + 1) * 256],
                      s=bank(zlb)[:, 0:256], i=linv(cid):
               e.tensor_tensor(out=d, in0=s, in1=i, op=ALU.mult),
               wbufs=[f"za{h}.{c}"], rbufs=[f"ps{zlb}", f"lin{cid}"],
               tag=f"ze{h}.{c}")

        def head_loads(h):
            kb = h % 2
            dma("sp", f"kv{kb}", k_half(kb), k_scr[h], wbuf=f"kbuf{kb}",
                waits=(p1_done if h < 2 else ()), tag=f"kl{h}")
            dma("sp", f"vv{kb}", v_half(kb).rearrange("p (t e) -> p t e", e=E),
                v_scr[h].rearrange("(t p) e -> p t e", p=128),
                wbuf=f"vbuf{kb}", tag=f"vl{h}")
            dma("sp", f"qq{kb}", q_slot(kb), q_scr[h], wbuf=f"qbuf{kb}", tag=f"ql{h}")

        head_loads(0)
        head_loads(1)
        for h in range(H):
            kb = h % 2
            hstate.clear()
            for duo in ((0, 1), (2, 3), (4, 5), (6, 7)):
                nq = {c: c + 1 for c in duo}
                pos = {c: 0 for c in duo}
                live = list(duo)
                sched = []
                while live:
                    for c in list(live):
                        sched.append((c, pos[c]))
                        pos[c] += 1
                        if pos[c] == nq[c]:
                            live.remove(c)
                win = deque()
                for (c, qq) in sched:
                    ss = emit_quad_S(h, kb, c, c % 2, qq, 4 * c + 4)
                    win.append((c, qq, ss))
                    if len(win) >= 3:
                        c2, q2, s2 = win.popleft()
                        emit_quad_AV(h, kb, c2, c2 % 2, q2, 4 * c2 + 4, s2)
                        if q2 == nq[c2] - 1:
                            emit_epi(h, c2, c2 % 2)
                while win:
                    c2, q2, s2 = win.popleft()
                    emit_quad_AV(h, kb, c2, c2 % 2, q2, 4 * c2 + 4, s2)
                    if q2 == nq[c2] - 1:
                        emit_epi(h, c2, c2 % 2)
            if h + 2 < H:
                head_loads(h + 2)
        if dbg:
            dma("gp", "st", dbg_out["dl"], r2f[:, 0:512], rbufs=["ldn0", "ldn1"], tag="dbg_l")

        grids_done = [("gw", p.cnt["gw"])]

        if dbg:
            dma("gp", "st", dbg_out["dq0o"], q_scr[0], tag="dbg_q")
            dma("gp", "st", dbg_out["dk0"], k_scr[0], tag="dbg_k")
            dma("gp", "st", dbg_out["dv0"], v_scr[0], tag="dbg_v")
            dma("gp", "st", dbg_out["dgup"], gup[0], waits=[("gw", p.cnt["gw"])], tag="dbg_gup")
            for hh in range(H):
                for qq in range(4):
                    pass
            dma("gp", "st", dbg_out["dz"], zall, rbufs=[f"za{hh}.{qq}" for hh in range(H) for qq in range(8)], tag="dbg_z")
        for dc in range(NDC):
            slot = dc % 2
            sr = wo.rearrange("(c p) n -> p c n", p=128)[:, :, dc * 128:(dc + 1) * 128]
            dma("gp", f"wsl{slot}", wslot(slot).rearrange("p (c e) -> p c e", c=H), sr,
                wbuf=f"wsb{slot}", tag=f"wo{dc}")
            dma("sp", "xrs", stg[:, 0:2048], xown[dc * 128:(dc + 1) * 128, :],
                wbuf="xres", waits=(deq_done if dc == 0 else ()), tag=f"xr{dc}")
            for tb in range(4):
                bi = tb % 4
                dst = bank(bi)
                for h in range(H):
                    rb = [f"za{h}.{2 * tb}", f"za{h}.{2 * tb + 1}"] if dc == 0 else []
                    if h in (0, H - 1):
                        rb.append(f"wsb{slot}")
                    mm(dst, wslot(slot)[:, h * 128:(h + 1) * 128],
                       zall[:, h * 2048 + tb * 512:h * 2048 + (tb + 1) * 512],
                       start=(h == 0), stop=(h == H - 1),
                       wbufs=([f"ps{bi}"] if h in (0, H - 1) else ()), rbufs=rb,
                       tag=f"o{dc}.{tb}.{h}")
                sl = tb % 2
                op("dve", "dve",
                   lambda e, d=qsf(sl), s=dst, b=bo_sb[:, dc:dc + 1],
                          x=stg[:, tb * 512:(tb + 1) * 512]:
                   e.scalar_tensor_tensor(out=d, in0=s, scalar=b, in1=x,
                                          op0=ALU.add, op1=ALU.add),
                   wbufs=[f"qs{sl}"], rbufs=[f"ps{bi}", "xres", "bo"], tag=f"r1{dc}.{tb}")
                wa = list(deq_done) if dc >= 10 else []
                op("act", "act",
                   lambda e, d=rbig[:, dc * 2048 + tb * 512:dc * 2048 + (tb + 1) * 512],
                          s=qsf(sl):
                   e.activation(out=d, in_=s, func=AF.Copy),
                   wbufs=[f"rb{dc}"], rbufs=[f"qs{sl}"], waits=wa,
                   tag=f"rc{dc}.{tb}")

        zall_read = [("pe", p.cnt["pe"])]

        for tb in range(4):
            sb_ = 4 + (tb % 2) * 2
            qb_ = sb_ + 1
            for dc in range(NDC):
                sl = dc % 2
                op("act", "act",
                   lambda e, d=qsb(sl), s=rbig[:, dc * 2048 + tb * 512:dc * 2048 + (tb + 1) * 512]:
                   e.activation(out=d, in_=s, func=AF.Square),
                   wbufs=[f"qs{sl}"], rbufs=[f"rb{dc}"], tag=f"sq{tb}.{dc}")
                mm(bank(sb_), onesm[:],
                   rbig[:, dc * 2048 + tb * 512:dc * 2048 + (tb + 1) * 512],
                   start=(dc == 0), stop=(dc == NDC - 1),
                   wbufs=([f"ps{sb_}"] if dc in (0, NDC - 1) else ()),
                   rbufs=["onesm", f"rb{dc}"], tag=f"sm{tb}.{dc}")
                mm(bank(qb_), onesm[:], qsb(sl),
                   start=(dc == 0), stop=(dc == NDC - 1),
                   wbufs=([f"ps{qb_}"] if dc in (0, NDC - 1) else ()),
                   rbufs=["onesm", f"qs{sl}"], tag=f"sv{tb}.{dc}")
            op("dve", "dve",
               lambda e, d=r2f[:, tb * 512:(tb + 1) * 512], s=bank(sb_):
               e.tensor_scalar_mul(out=d, in0=s, scalar1=1.0 / 2048.0),
               wbufs=[f"lnm{tb}"], rbufs=[f"ps{sb_}"], tag=f"m1.{tb}")
            op("dve", "dve",
               lambda e, d=r2f[:, 2048 + tb * 512:2048 + (tb + 1) * 512], s=bank(qb_):
               e.tensor_scalar_mul(out=d, in0=s, scalar1=1.0 / 2048.0),
               wbufs=[f"lnv{tb}"], rbufs=[f"ps{qb_}"], tag=f"v1.{tb}")
        lnm = [f"lnm{t}" for t in range(4)]
        lnv = [f"lnv{t}" for t in range(4)]
        op("dve", "dve",
           lambda e: e.tensor_tensor(out=stg[:, 0:2048], in0=r2f[:, 0:2048],
                                     in1=r2f[:, 0:2048], op=ALU.mult),
           wbufs=["xres"], rbufs=lnm, tag="mm1")
        op("dve", "dve",
           lambda e: e.tensor_sub(out=r2f[:, 2048:4096], in0=r2f[:, 2048:4096],
                                  in1=stg[:, 0:2048]),
           wbufs=lnv, rbufs=["xres"], waits=sum((B[b].rd_waits() for b in lnv), []),
           tag="var1")
        op("act", "act",
           lambda e: e.activation(out=r2f[:, 2048:4096], in_=r2f[:, 2048:4096],
                                  func=AF.Ln, bias=epsb[:]),
           wbufs=lnv, rbufs=["epsb"],
           waits=sum((B[b].wr_waits() for b in lnv), []), tag="ln1a")
        op("act", "act",
           lambda e: e.activation(out=r2f[:, 2048:4096], in_=r2f[:, 2048:4096],
                                  func=AF.Exp, scale=-0.5),
           wbufs=lnv, waits=sum((B[b].wr_waits() for b in lnv), []), tag="ln1b")
        op("dve", "dve",
           lambda e: e.scalar_tensor_tensor(out=stg[:, 0:2048], in0=r2f[:, 0:2048],
                                            scalar=-1.0, in1=r2f[:, 2048:4096],
                                            op0=ALU.mult, op1=ALU.mult),
           wbufs=["xres"], rbufs=lnm + lnv, tag="brow1")
        if dbg:
            dma("gp", "st", dbg_out["dr"], rbig, rbufs=[f"rb{d_}" for d_ in range(NDC)], tag="dbg_r")
        aslot = stg[:, 2048:3072].bitcast(BF16)[:, 0:2048]
        for dc in range(NDC):
            op("po", "po",
               lambda e, s=rbig[:, dc * 2048:(dc + 1) * 2048]:
               e.tensor_tensor(out=aslot, in0=s, in1=r2f[:, 2048:4096], op=ALU.mult),
               wbufs=["qs0", "qs1"], rbufs=[f"rb{dc}"] + lnv, tag=f"lA{dc}")
            op("po", "po",
               lambda e, d=xlnb[:, dc * 2048:(dc + 1) * 2048]:
               e.tensor_tensor(out=d, in0=aslot, in1=stg[:, 0:2048], op=ALU.add),
               wbufs=[f"xl{dc}"], rbufs=["qs0", "qs1", "xres"],
               waits=zall_read, tag=f"lB{dc}")

        upslot = lambda s: wsb[:, s * 2048:(s + 1) * 2048]
        dnslot = lambda s: wsb[:, (2 + s) * 2048:(3 + s) * 2048]
        def ln2_stat(tg, dc):
            sl = dc % 2
            op("act", "act",
               lambda e, d=qsb(sl), s=r2b[:, dc * 512:(dc + 1) * 512]:
               e.activation(out=d, in_=s, func=AF.Square),
               wbufs=[f"qs{sl}"], rbufs=[f"r2{dc}"], tag=f"s2{tg}.{dc}")
            mm(bank(6), onesm[:], r2b[:, dc * 512:(dc + 1) * 512],
               start=(dc == 0), stop=(dc == NDC - 1),
               wbufs=(["ps6"] if dc in (0, NDC - 1) else ()), rbufs=["onesm", f"r2{dc}"],
               tag=f"sm2.{tg}.{dc}")
            mm(bank(7), onesm[:], qsb(sl),
               start=(dc == 0), stop=(dc == NDC - 1),
               wbufs=(["ps7"] if dc in (0, NDC - 1) else ()), rbufs=["onesm", f"qs{sl}"],
               tag=f"sv2.{tg}.{dc}")

        for tg in range(4):
            for c in range(NDC):
                mm(bank(3)[0:R, :], aup_sb[:, c * R:(c + 1) * R],
                   xlnb[:, c * 2048 + tg * 512:c * 2048 + (tg + 1) * 512],
                   start=(c == 0), stop=(c == NDC - 1),
                   wbufs=(["ps3"] if c in (0, NDC - 1) else ()), rbufs=["aup", f"xl{c}"],
                   tag=f"tu{tg}.{c}")
            op("dve", "dve", lambda e: e.tensor_copy(out=tup_sb[:], in_=bank(3)[0:R, :]),
               wbufs=["tupb"], rbufs=["ps3"], tag=f"tue{tg}")
            for hc in range(NHC):
                s = hc % 2
                dma("sp", f"wsl{s}", upslot(s).rearrange("p (k e) -> p k e", k=NDC),
                    gup[hc], wbuf=f"wsb{s}",
                    waits=(grids_done if (tg == 0 and hc < 2) else ()),
                    tag=f"gu{tg}.{hc}")
                if hc % 8 == 0:
                    bl = (hc // 8) % 2
                    dma("gp", f"bl{bl}",
                        bupl_sb[:, bl * 1024:(bl + 1) * 1024],
                        buplw[:, hc * 128:(hc + 8) * 128],
                        wbuf=f"bupl{bl}", tag=f"bl{tg}.{hc}")
                bi = hc % 2
                dst = bank(bi)
                for c in range(NDC):
                    rb = [f"wsb{s}"] if c in (0, NDC - 1) else []
                    mm(dst, upslot(s)[:, c * 128:(c + 1) * 128],
                       xlnb[:, c * 2048 + tg * 512:c * 2048 + (tg + 1) * 512],
                       start=(c == 0), stop=False,
                       wbufs=([f"ps{bi}"] if c == 0 else ()), rbufs=rb,
                       tag=f"u{tg}.{hc}.{c}")
                bl = (hc // 8) % 2
                mm(dst, bupl_sb[:, bl * 1024 + (hc % 8) * 128:bl * 1024 + (hc % 8 + 1) * 128],
                   tup_sb[:], start=False, stop=True, wbufs=[f"ps{bi}"],
                   rbufs=[f"bupl{bl}", "tupb"], tag=f"ul{tg}.{hc}")
                wx = [f"x2{hc}"] + ([f"rb{hc // 4}"] if tg == 0 else [])
                op("act", "act",
                   lambda e, d=x2b[:, hc * 512:(hc + 1) * 512], s=dst, b=bup_sb[:, hc:hc + 1]:
                   e.activation(out=d, in_=s, func=AF.Relu, bias=b, scale=1.0),
                   wbufs=wx, rbufs=[f"ps{bi}", "bup"], tag=f"re{tg}.{hc}")
                if hc >= 1:
                    h2 = hc - 1
                    mm(bank(3)[0:R, :], adn_sb[:, h2 * R:(h2 + 1) * R],
                       x2b[:, h2 * 512:(h2 + 1) * 512],
                       start=(h2 == 0), stop=False,
                       wbufs=(["ps3"] if h2 == 0 else ()), rbufs=["adn", f"x2{h2}"],
                       tag=f"t2{tg}.{h2}")
                if hc == NHC - 1:
                    mm(bank(3)[0:R, :], adn_sb[:, hc * R:(hc + 1) * R],
                       x2b[:, hc * 512:(hc + 1) * 512],
                       start=False, stop=True,
                       wbufs=["ps3"], rbufs=["adn", f"x2{hc}"],
                       tag=f"t2{tg}.{hc}")
            if dbg and tg == 0:
                dma("gp", "st", dbg_out["dxl"], xlnb,
                    rbufs=[f"xl{d_}" for d_ in range(NDC)], tag="dbg_xl")
                dma("gp", "st", dbg_out["dx2"], x2b,
                    rbufs=[f"x2{h_}" for h_ in range(NHC)], tag="dbg_x2")
            op("dve", "dve", lambda e: e.tensor_copy(out=t2_sb[:], in_=bank(3)[0:R, :]),
               wbufs=["t2b"], rbufs=["ps3"], tag=f"t2e{tg}")
            for dc in range(NDC):
                bi = 4 + (dc % 2)
                dst = bank(bi)
                for qt in range(4):
                    s = qt % 2
                    dma("sp", f"wsl{2 + s}", dnslot(s).rearrange("p (k e) -> p k e", k=16),
                        gdn[dc, qt],
                        wbuf=f"wsb{2 + s}",
                        waits=(grids_done if (tg == 0 and dc == 0 and qt < 2) else ()),
                        tag=f"gd{tg}.{dc}.{qt}")
                    for kk in range(16):
                        hc = qt * 16 + kk
                        rb = [f"x2{hc}"] if dc in (0, NDC - 1) else []
                        if kk in (0, 15):
                            rb.append(f"wsb{2 + s}")
                        mm(dst, dnslot(s)[:, kk * 128:(kk + 1) * 128],
                           x2b[:, hc * 512:(hc + 1) * 512],
                           start=(hc == 0), stop=False,
                           wbufs=([f"ps{bi}"] if hc == 0 else ()), rbufs=rb,
                           tag=f"d{tg}.{dc}.{hc}")
                mm(dst, bdnl_sb[:, dc * 128:(dc + 1) * 128], t2_sb[:],
                   start=False, stop=True, wbufs=[f"ps{bi}"],
                   rbufs=["bdnl", "t2b"], tag=f"dl{tg}.{dc}")
                op("dve", "dve",
                   lambda e, d=r2b[:, dc * 512:(dc + 1) * 512], s=dst,
                          b=bdn_sb[:, dc:dc + 1],
                          x=xlnb[:, dc * 2048 + tg * 512:dc * 2048 + (tg + 1) * 512]:
                   e.scalar_tensor_tensor(out=d, in0=s, scalar=b, in1=x,
                                          op0=ALU.add, op1=ALU.add),
                   wbufs=[f"r2{dc}"], rbufs=[f"ps{bi}", "bdn", f"xl{dc}"],
                   tag=f"r2.{tg}.{dc}")
                if dc >= 1:
                    ln2_stat(tg, dc - 1)
            ln2_stat(tg, NDC - 1)
            op("dve", "dve",
               lambda e: e.tensor_scalar_mul(out=stg[:, 0:512], in0=bank(6),
                                             scalar1=1.0 / 2048.0),
               wbufs=["l2m"], rbufs=["ps6"], tag=f"m2.{tg}")
            op("dve", "dve",
               lambda e: e.tensor_scalar_mul(out=stg[:, 512:1024], in0=bank(7),
                                             scalar1=1.0 / 2048.0),
               wbufs=["l2v"], rbufs=["ps7"], tag=f"v2.{tg}")
            op("dve", "dve",
               lambda e: e.tensor_tensor(out=stg[:, 1024:1536], in0=stg[:, 0:512],
                                         in1=stg[:, 0:512], op=ALU.mult),
               wbufs=["l2t"], rbufs=["l2m"], tag=f"mm2.{tg}")
            op("dve", "dve",
               lambda e: e.tensor_sub(out=stg[:, 512:1024], in0=stg[:, 512:1024],
                                      in1=stg[:, 1024:1536]),
               wbufs=["l2v"], rbufs=["l2t"], waits=B["l2v"].wr_waits(), tag=f"va2.{tg}")
            op("act", "act",
               lambda e: e.activation(out=stg[:, 512:1024], in_=stg[:, 512:1024],
                                      func=AF.Ln, bias=epsb[:]),
               wbufs=["l2v"], rbufs=["epsb"], waits=B["l2v"].wr_waits(), tag=f"lna2.{tg}")
            op("act", "act",
               lambda e: e.activation(out=stg[:, 512:1024], in_=stg[:, 512:1024],
                                      func=AF.Exp, scale=-0.5),
               wbufs=["l2v"], waits=B["l2v"].wr_waits(), tag=f"lnb2.{tg}")
            op("dve", "dve",
               lambda e: e.scalar_tensor_tensor(out=stg[:, 1024:1536], in0=stg[:, 0:512],
                                                scalar=-1.0, in1=stg[:, 512:1024],
                                                op0=ALU.mult, op1=ALU.mult),
               wbufs=["l2t"], rbufs=["l2m", "l2v"], tag=f"br2.{tg}")
            for dc in range(NDC):
                sl = dc % 2
                op("po", "po",
                   lambda e, d=qsf(sl), s=r2b[:, dc * 512:(dc + 1) * 512]:
                   e.tensor_tensor(out=d, in0=s, in1=stg[:, 512:1024], op=ALU.mult),
                   wbufs=[f"qs{sl}"], rbufs=[f"r2{dc}", "l2v"], tag=f"fA{tg}.{dc}")
                op("po", "po",
                   lambda e, d=qsf(sl):
                   e.tensor_add(out=d, in0=d, in1=stg[:, 1024:1536]),
                   wbufs=[f"qs{sl}"], rbufs=["l2t"], waits=B[f"qs{sl}"].wr_waits(),
                   tag=f"fB{tg}.{dc}")
                dma("gp", "st", out[dc * 128:(dc + 1) * 128, tg * 512:(tg + 1) * 512],
                    qsf(sl), rbufs=[f"qs{sl}"], tag=f"w{tg}.{dc}")

        st_total = p.cnt["st"]
        p.emit("sp", lambda e: e.nop(), [("st", st_total)], [], "fin_sp")
        p.emit("gp", lambda e: e.nop(), [("st", st_total)], [], "fin_gp")

        p.check()

        nc.all_engine_barrier()
        for s_ in sems.values():
            nc.gpsimd.sem_clear(s_)
        nc.all_engine_barrier()

        with nc.Block() as block:
            def runner(key):
                def body(eng):
                    for e_, fn, ws, incs, _tag in p.ops:
                        if e_ != key:
                            continue
                        for s, v in ws:
                            eng.wait_ge(sems[s], v)
                        inst = fn(eng)
                        for s, n in incs:
                            inst.then_inc(sems[s], n)
                return body
            block.tensor(runner("pe"))
            block.scalar(runner("act"))
            block.vector(runner("dve"))
            block.gpsimd(runner("gp"))
            block.sync(runner("sp"))

    return nc


_NC_CACHE = {}


def _get_nc():
    if "nc" not in _NC_CACHE:
        _NC_CACHE["nc"] = build_program()
    return _NC_CACHE["nc"]


def kernel(L0_attn_x, L0_attn_W_Q, L0_attn_W_K, L0_attn_W_V, L0_attn_W_O,
           L0_attn_b_Q, L0_attn_b_K, L0_attn_b_V, L0_attn_b_O,
           L0_ffn_w_up_q, L0_ffn_w_up_scale, L0_ffn_b_up,
           L0_ffn_w_up_lora_a, L0_ffn_w_up_lora_b,
           L0_ffn_w_down_q, L0_ffn_w_down_scale, L0_ffn_b_down,
           L0_ffn_w_down_lora_a, L0_ffn_w_down_lora_b):
    x = np.asarray(L0_attn_x, dtype=np.float32)
    nb = x.shape[0]
    f32 = lambda a: np.ascontiguousarray(np.asarray(a, dtype=np.float32))

    wq_l = f32(np.asarray(L0_attn_W_Q).transpose(1, 0, 2).reshape(D, H * E))
    wk_l = f32(np.asarray(L0_attn_W_K).transpose(1, 0, 2).reshape(D, H * E))
    wv_l = f32(np.asarray(L0_attn_W_V).transpose(1, 0, 2).reshape(D, H * E))
    wo_l = f32(np.asarray(L0_attn_W_O).reshape(H * E, D))
    bq_l = f32(np.asarray(L0_attn_b_Q).T)
    bk_l = f32(np.asarray(L0_attn_b_K).T)
    bv_l = f32(np.asarray(L0_attn_b_V).reshape(1, H * E))
    bo_l = f32(np.asarray(L0_attn_b_O).reshape(NDC, 128).T)
    qup_l = np.ascontiguousarray(np.asarray(L0_ffn_w_up_q, dtype=np.int32).T)
    sup_l = f32(np.asarray(L0_ffn_w_up_scale).T)
    qdn_l = np.ascontiguousarray(np.asarray(L0_ffn_w_down_q, dtype=np.int32).T)
    sdn_l = f32(np.asarray(L0_ffn_w_down_scale).T)
    sel_l = np.zeros((2, 128), np.float32)
    sel_l[0, 0:64] = 1.0
    sel_l[1, 64:128] = 1.0
    bup_l = f32(np.asarray(L0_ffn_b_up).reshape(NHC, 128).T)
    bdn_l = f32(np.asarray(L0_ffn_b_down).reshape(NDC, 128).T)
    aup_l = f32(L0_ffn_w_up_lora_a)
    bupl_l = f32(L0_ffn_w_up_lora_b)
    adn_l = f32(L0_ffn_w_down_lora_a)
    bdnl_l = f32(L0_ffn_w_down_lora_b)

    NEG = np.float32(-1e9)
    triu = (np.arange(128)[:, None] > np.arange(128)[None, :]).astype(np.float32) * NEG
    allm = np.full((128, 128), NEG, np.float32)
    zero = np.zeros((128, 128), np.float32)
    msk_p = [np.concatenate([triu, allm], axis=1),
             np.concatenate([zero, triu], axis=1)]
    idn_l = np.eye(128, dtype=np.float32)

    in_maps = []
    for c in range(8):
        b = c // 2
        par = c % 2
        xb = x[b]
        xfm_c = np.ascontiguousarray(xb.T)
        xown_c = np.ascontiguousarray(
            xb.reshape(32, 128, D)[par::2].reshape(TOWN, D).T)
        in_maps.append(dict(
            xfm=xfm_c, xown=xown_c, wq=wq_l, wk=wk_l, wv=wv_l, wo=wo_l,
            bq=bq_l, bk=bk_l, bv=bv_l, bo=bo_l, msk=np.ascontiguousarray(msk_p[par]),
            qupt=qup_l, supt=sup_l, qdnt=qdn_l, sdnt=sdn_l, sel2=sel_l, idn=idn_l,
            bup=bup_l, bdn=bdn_l, aupw=aup_l, buplw=bupl_l, adnw=adn_l,
            bdnlw=bdnl_l))

    nc = _get_nc()
    res = run_bass_kernel_spmd(nc, in_maps, core_ids=list(range(8)))

    y = np.empty((nb, T, D), dtype=np.float32)
    for c in range(8):
        b = c // 2
        par = c % 2
        o = res.results[c]["out"]
        y[b].reshape(32, 128, D)[par::2] = o.T.reshape(16, 128, D)
    return y


# version 18
# speedup vs baseline: 1.1880x; 1.1880x over previous
import numpy as np
from collections import defaultdict, deque
from contextlib import ExitStack

import concourse.bass as bass
import concourse.mybir as mybir
from concourse.bass_utils import run_bass_kernel_spmd

F32 = mybir.dt.float32
BF16 = mybir.dt.bfloat16
I32 = mybir.dt.int32
AF = mybir.ActivationFunctionType
ALU = mybir.AluOpType

D = 2048
T = 4096
TOWN = 2048
H = 16
E = 128
HF = 8192
R = 16
NDC = D // 128
NHC = HF // 128
ISCALE = float(1.0 / np.sqrt(np.float32(128.0)))
LNEPS = 1e-5

ENGS = ("pe", "act", "dve", "sp", "gp")
STREAM = {"pe": "pe", "act": "act", "ald": "act", "dve": "dve",
          "sp": "sp", "gp": "gp", "po": "gp"}


class Prog:
    def __init__(self):
        self.ops = []
        self.cnt = defaultdict(int)

    def emit(self, stream, fn, waits=(), incs=(), tag=""):
        w = {}
        for s, v in waits:
            if v > 0:
                w[s] = max(w.get(s, 0), v)
        ws = sorted(w.items())
        self.ops.append((STREAM[stream], fn, ws, list(incs), tag))
        out = []
        for s, n in incs:
            self.cnt[s] += n
            out.append((s, self.cnt[s]))
        return out

    def check(self):
        qs = {e: deque() for e in ENGS}
        for opr in self.ops:
            qs[opr[0]].append(opr)
        cnt = defaultdict(int)
        total = len(self.ops)
        done = 0
        while done < total:
            prog = False
            for e in ENGS:
                while qs[e]:
                    _, fn, ws, incs, tag = qs[e][0]
                    if all(cnt[s] >= v for s, v in ws):
                        for s, n in incs:
                            cnt[s] += n
                        qs[e].popleft()
                        done += 1
                        prog = True
                    else:
                        break
            if not prog:
                blocked = {e: (qs[e][0][4], [(s, v, cnt[s]) for s, v in qs[e][0][2]
                                             if cnt[s] < v])
                           for e in ENGS if qs[e]}
                raise RuntimeError(f"schedule deadlock: {blocked}")


class Buf:
    __slots__ = ("ready", "readers")

    def __init__(self):
        self.ready = {}
        self.readers = {}

    @staticmethod
    def _merge(d, ms):
        for s, v in ms:
            d[s] = max(d.get(s, 0), v)

    def wr_waits(self):
        return list(self.readers.items()) + list(self.ready.items())

    def rd_waits(self):
        return list(self.ready.items())

    def set_written(self, ms):
        self.ready = dict(ms)
        self.readers = {}

    def add_reader(self, ms):
        self._merge(self.readers, ms)


def build_program(dbg=False):
    nc = bass.Bass("TRN2", target_bir_lowering=False, debug=False)
    dri = lambda name, shape, dt=F32: nc.dram_tensor(name, shape, dt, kind="ExternalInput").ap()

    xfm = dri("xfm", [D, T])
    xown = dri("xown", [D, TOWN])
    wq = dri("wq", [D, H * E])
    wk = dri("wk", [D, H * E])
    wv = dri("wv", [D, H * E])
    wo = dri("wo", [H * E, D])
    bq = dri("bq", [E, H])
    bk = dri("bk", [E, H])
    bv = dri("bv", [1, H * E])
    bo = dri("bo", [128, NDC])
    msk = dri("msk", [128, 256])
    qupt = dri("qupt", [D, HF], I32)
    supt = dri("supt", [32, HF])
    qdnt = dri("qdnt", [HF, D], I32)
    sdnt = dri("sdnt", [128, D])
    sel2 = dri("sel2", [2, 128])
    idn = dri("idn", [128, 128])
    bup = dri("bup", [128, NHC])
    bdn = dri("bdn", [128, NDC])
    aupw = dri("aupw", [D, R])
    buplw = dri("buplw", [R, HF])
    adnw = dri("adnw", [HF, R])
    bdnlw = dri("bdnlw", [R, D])

    out = nc.dram_tensor("out", [D, TOWN], F32, kind="ExternalOutput").ap()

    dbg_out = {}
    if dbg:
        for nm, shape, dt in [("dq0o", [E, TOWN], BF16), ("dk0", [E, T], BF16),
                              ("dv0", [T, E], BF16), ("dz", [128, 32768], BF16),
                              ("dr", [128, 32768], BF16), ("dxl", [128, 32768], BF16),
                              ("dx2", [128, 32768], BF16), ("dup", [HF, 128], BF16),
                              ("dgup", [128, NDC, 128], BF16), ("dl", [128, 512], F32)]:
            dbg_out[nm] = nc.dram_tensor(nm, shape, dt, kind="ExternalOutput").ap()
    q_scr = nc.dram_tensor("q_scr", [H, E, TOWN], BF16, kind="Internal").ap()
    k_scr = nc.dram_tensor("k_scr", [H, E, T], BF16, kind="Internal").ap()
    v_scr = nc.dram_tensor("v_scr", [H, T, E], BF16, kind="Internal").ap()
    bupl_scr = nc.dram_tensor("bupl_scr", [R, HF], BF16, kind="Internal").ap()
    gup = nc.dram_tensor("gup", [NHC, 128, NDC, 128], BF16, kind="Internal").ap()
    gdn = nc.dram_tensor("gdn", [NDC, 4, 128, 16, 128], BF16, kind="Internal").ap()

    p = Prog()
    B = defaultdict(Buf)

    with ExitStack() as ctx:
        en = ctx.enter_context
        arena = en(nc.sbuf_tensor("arena", [128, 131072], mybir.dt.uint8))
        a_bf = arena[:].bitcast(BF16)
        xh_bf = a_bf[:, 0:32768]
        zall = a_bf[:, 0:32768]
        xlnb = a_bf[:, 0:32768]
        rbig = a_bf[:, 32768:65536]
        x2b = a_bf[:, 32768:65536]
        p_sb = a_bf[:, 32768:36864]
        kq0 = 36864
        kv_bf = a_bf[:, kq0:kq0 + 16384]
        q0 = kq0 + 16384
        q_bf2 = a_bf[:, q0:q0 + 4096]
        xh_up = a_bf[:, 32768:65536]

        wsb = en(nc.sbuf_tensor("wsb", [128, 8192], BF16))
        stg = en(nc.sbuf_tensor("stg", [128, 3072], F32))
        r2b = en(nc.sbuf_tensor("r2b", [128, 8192], BF16))
        r2f = r2b[:].bitcast(F32)
        dq_in_s = [r2f[:, 2048:3072].bitcast(BF16)[:, 0:2048],
                   r2f[:, 3072:4096].bitcast(BF16)[:, 0:2048]]
        dq_out_s = [stg[:, 0:1024].bitcast(BF16)[:, 0:2048],
                    stg[:, 1024:2048].bitcast(BF16)[:, 0:2048]]
        mskb = en(nc.sbuf_tensor("mskb", [128, 256], BF16))

        onesm = en(nc.sbuf_tensor("onesm", [128, 128], BF16))
        onesr = en(nc.sbuf_tensor("onesr", [1, 128], BF16))
        bq_sb = en(nc.sbuf_tensor("bq_sb", [128, H], F32))
        bk_sb = en(nc.sbuf_tensor("bk_sb", [128, H], F32))
        bv_sb = en(nc.sbuf_tensor("bv_sb", [1, H * E], BF16))
        bo_sb = en(nc.sbuf_tensor("bo_sb", [128, NDC], F32))
        bup_sb = en(nc.sbuf_tensor("bup_sb", [128, NHC], F32))
        bdn_sb = en(nc.sbuf_tensor("bdn_sb", [128, NDC], F32))
        aup_sb = en(nc.sbuf_tensor("aup_sb", [128, NDC * R], BF16))
        adn_sb = en(nc.sbuf_tensor("adn_sb", [128, NHC * R], BF16))
        bupl_sb = en(nc.sbuf_tensor("bupl_sb", [R, 2048], BF16))
        bdnl_sb = en(nc.sbuf_tensor("bdnl_sb", [R, D], BF16))
        tup_sb = en(nc.sbuf_tensor("tup_sb", [R, 512], BF16))
        t2_sb = en(nc.sbuf_tensor("t2_sb", [R, 512], BF16))
        srow_sb = en(nc.sbuf_tensor("srow_sb", [2, 4096], BF16))
        sel_sb = en(nc.sbuf_tensor("sel_sb", [2, 128], BF16))
        idn_sb = en(nc.sbuf_tensor("idn_sb", [128, 128], BF16))
        epsb = en(nc.sbuf_tensor("epsb", [128, 1], F32))

        psAB = en(nc.psum_tensor("psAB", [128, 2048], F32))
        psCD = en(nc.psum_tensor("psCD", [128, 1024], F32))
        psEF = en(nc.psum_tensor("psEF", [128, 1024], F32))

        def bank(i, n=1):
            if i + n <= 4:
                return psAB[:, i * 512:(i + n) * 512]
            if 4 <= i and i + n <= 6:
                return psCD[:, (i - 4) * 512:(i - 4 + n) * 512]
            assert 6 <= i and i + n <= 8
            return psEF[:, (i - 6) * 512:(i - 6 + n) * 512]

        sems = {}
        for s in ("cst", "cstg", "xh0", "xh1", "wsl0", "wsl1", "wsl2", "wsl3",
                  "qw0", "qw1", "qw2", "qw3", "dqi0", "dqi1", "srw0", "srw1", "gw", "xrs",
                  "kv0", "kv1", "vv0", "vv1", "qq0", "qq1", "bl0", "bl1",
                  "st", "pe", "act", "dve", "po"):
            sems[s] = en(nc.semaphore(s))

        def op(stream, sem, fn, wbufs=(), rbufs=(), waits=(), tag="", inc=1):
            ws = list(waits)
            for wb in wbufs:
                ws += B[wb].wr_waits()
            for rb in rbufs:
                ws += B[rb].rd_waits()
            ms = p.emit(stream, fn, ws, [(sem, inc)], tag)
            for wb in wbufs:
                B[wb].set_written(ms)
            for rb in rbufs:
                B[rb].add_reader(ms)
            return ms

        def dma(stream, sem, dst, src, wbuf=None, rbufs=(), waits=(), tag="", tr=False):
            fn = (lambda e, d=dst, s=src: e.dma_start(out=d, in_=s, transpose=True)) if tr \
                else (lambda e, d=dst, s=src: e.dma_start(out=d, in_=s))
            return op(stream, sem, fn, wbufs=([wbuf] if wbuf else ()), rbufs=rbufs,
                      waits=waits, tag=tag, inc=16)

        def mm(dst, lhsT, rhs, start, stop, wbufs=(), rbufs=(), waits=(), tag=""):
            return op("pe", "pe",
                      lambda e, d=dst, l=lhsT, r=rhs, st=start, sp=stop:
                      e.matmul(d, l, r, start=st, stop=sp),
                      wbufs, rbufs, waits, tag)

        def batch_seal(sem, bufs):
            tot = p.cnt[sem]
            for b_ in bufs:
                B[b_].ready = {sem: tot}
        wslot = lambda k: wsb[:, k * 2048:(k + 1) * 2048]
        def qsf(k):
            if k < 2:
                return stg[:, 2048 + k * 512:2048 + (k + 1) * 512]
            return r2f[:, 1024 + (k - 2) * 512:1024 + (k - 1) * 512]
        qsb = lambda k: qsf(k).bitcast(BF16)[:, 0:512]

        op("dve", "dve", lambda e: e.memset(onesm[:], 1.0), wbufs=["onesm"], tag="ones")
        op("dve", "dve", lambda e: e.memset(onesr[:], 1.0), wbufs=["onesr"], tag="onesr")
        op("dve", "dve", lambda e: e.memset(epsb[:], LNEPS), wbufs=["epsb"], tag="eps")
        dma("sp", "cst", bq_sb[:], bq, wbuf="bq", tag="bq")
        dma("sp", "cst", bk_sb[:], bk, wbuf="bk", tag="bk")
        dma("sp", "cst", bo_sb[:], bo, wbuf="bo", tag="bo")
        dma("sp", "cst", bup_sb[:], bup, wbuf="bup", tag="bup")
        dma("sp", "cst", bdn_sb[:], bdn, wbuf="bdn", tag="bdn")
        dma("gp", "cstg", bv_sb[:], bv, wbuf="bv", tag="bv")
        dma("gp", "cstg", mskb[:], msk, wbuf="msk", tag="msk")
        dma("gp", "cstg", aup_sb[:].rearrange("p (c r) -> p c r", c=NDC),
            aupw.rearrange("(c p) r -> p c r", p=128), wbuf="aup", tag="aup")
        dma("gp", "cstg", adn_sb[:].rearrange("p (c r) -> p c r", c=NHC),
            adnw.rearrange("(c p) r -> p c r", p=128), wbuf="adn", tag="adn")
        dma("gp", "cstg", bdnl_sb[:], bdnlw, wbuf="bdnl", tag="bdnl")
        dma("gp", "cstg", bupl_scr, buplw, wbuf="buplscr", tag="buplscr")
        batch_seal("cst", ["bq", "bk", "bo", "bup", "bdn"])
        batch_seal("cstg", ["bv", "msk", "aup", "adn", "bdnl"])

        def load_whead(src, h, slot, tag):
            sr = src.rearrange("(c p) n -> p c n", p=128)[:, :, h * E:(h + 1) * E]
            return dma("gp", f"wsl{slot}", wslot(slot).rearrange("p (c e) -> p c e", c=NDC),
                       sr, wbuf=f"wsb{slot}", tag=tag)

        dma("gp", "cstg", sel_sb[:], sel2, wbuf="sel", tag="sel2")
        dma("gp", "cstg", idn_sb[:], idn, wbuf="idn", tag="idn")
        dq_jobs = [("u", dc, hq) for dc in range(NDC) for hq in range(4)] + \
                  [("d", hc, 0) for hc in range(NHC)]
        dq_state = {"lc": 0, "cc": 0}

        def _dq_aps(i):
            kind, a1, a2 = dq_jobs[i]
            if kind == "u":
                dc, hq = a1, a2
                return (qupt[dc * 128:(dc + 1) * 128, hq * 2048:(hq + 1) * 2048],
                        supt[2 * dc:2 * dc + 2, hq * 2048:(hq + 1) * 2048],
                        gup[hq * 16:(hq + 1) * 16, :, dc, :], f"u{dc}.{hq}")
            hc = a1
            return (qdnt[hc * 128:(hc + 1) * 128, :],
                    sdnt[2 * hc:2 * hc + 2, :],
                    gdn[:, hc // 16, :, hc % 16, :], f"d{hc}")

        def _dq_load(i):
            s = i % 2
            qa, sa, _, tg = _dq_aps(i)
            dma("gp", f"dqi{s}", dq_in_s[s], qa, wbuf=f"dqin{s}", tag=f"{tg}i")
            dma("gp", f"srw{s}", srow_sb[:, s * 2048:(s + 1) * 2048], sa,
                wbuf=f"srow{s}", tag=f"{tg}s")

        def _dq_compute(i):
            s = i % 2
            _, _, ga, tg = _dq_aps(i)
            for pc in range(4):
                bk_i = 4 + pc
                mm(bank(bk_i), sel_sb[:],
                   srow_sb[:, s * 2048 + pc * 512:s * 2048 + (pc + 1) * 512],
                   start=True, stop=True,
                   wbufs=[f"ps{bk_i}"],
                   rbufs=(["sel", f"srow{s}"] if pc in (0, 3) else [f"srow{s}"]),
                   tag=f"{tg}.b{pc}")
                op("dve", "dve",
                   lambda e, o=dq_out_s[s][:, pc * 512:(pc + 1) * 512],
                          ii=dq_in_s[s][:, pc * 512:(pc + 1) * 512],
                          sr=bank(bk_i):
                   e.scalar_tensor_tensor(out=o, in0=ii, scalar=-7.5, in1=sr,
                                          op0=ALU.add, op1=ALU.mult),
                   wbufs=([f"dqout{s}"] if pc in (0, 3) else ()),
                   rbufs=[f"dqin{s}" if pc == 0 else f"dqout{s}", f"ps{bk_i}"],
                   tag=f"{tg}.q{pc}")
            dma("sp", "gw", ga.rearrange("k p e -> p k e"),
                dq_out_s[s].rearrange("p (k e) -> p k e", k=16),
                rbufs=[f"dqout{s}"], tag=f"{tg}w")

        def run_dq(n):
            for _ in range(n):
                if dq_state["lc"] < min(dq_state["cc"] + 2, len(dq_jobs)):
                    _dq_load(dq_state["lc"])
                    dq_state["lc"] += 1
                if dq_state["cc"] < dq_state["lc"]:
                    _dq_compute(dq_state["cc"])
                    dq_state["cc"] += 1

        load_whead(wq, 0, 0, "wq0pre")
        load_whead(wq, 1, 1, "wq1pre")
        for c in range(NDC):
            dma("gp", f"xh{c % 2}", xh_bf[:, c * 2048:(c + 1) * 2048],
                xown[c * 128:(c + 1) * 128, :], wbuf=f"xq{c}",
                waits=[(f"xh{c % 2}", 16 * (c // 2))], tag=f"xo{c}")

        def xup_load(c):
            dma("gp", f"xh{c % 2}", xh_up[:, c * 2048:(c + 1) * 2048],
                xfm[c * 128:(c + 1) * 128, 0:2048], wbuf=f"xa{c}",
                waits=[(f"xh{c % 2}", 16 * (8 + c // 2))], tag=f"xfa{c}")

        for h in range(H):
            slot = h % 4
            if h + 2 < H:
                load_whead(wq, h + 2, (h + 2) % 4, f"wq{h + 2}")
            for qb in range(4):
                bi = (h * 4 + qb) % 4
                dst = bank(bi)
                for c in range(NDC):
                    rb = []
                    if h == 0 and qb == 0:
                        rb.append(f"xq{c}")
                    if h == H - 1 and qb == 3:
                        rb.append(f"xq{c}")
                    if c in (0, NDC - 1):
                        rb.append(f"wsb{slot}")
                    mm(dst, wslot(slot)[:, c * E:(c + 1) * E],
                       xh_bf[:, c * 2048 + qb * 512:c * 2048 + (qb + 1) * 512],
                       start=(c == 0), stop=(c == NDC - 1),
                       wbufs=([f"ps{bi}"] if c in (0, NDC - 1) else ()), rbufs=rb,
                       tag=f"q{h}.{qb}.{c}")
                sl = (h * 4 + qb) % 4
                op("act", "act",
                   lambda e, d=qsb(sl), s=dst, b=bq_sb[:, h:h + 1]:
                   e.activation(out=d, in_=s, func=AF.Identity, bias=b, scale=1.0),
                   wbufs=[f"qs{sl}"], rbufs=[f"ps{bi}", "bq"], tag=f"qe{h}.{qb}")
                dma("sp", f"qw{sl}", q_scr[h, :, qb * 512:(qb + 1) * 512], qsb(sl),
                    rbufs=[f"qs{sl}"], tag=f"qw{h}.{qb}")
            run_dq(2)
            xup_load(h)

        def xlo_load(c):
            dma("gp", f"xh{c % 2}", xh_bf[:, c * 2048:(c + 1) * 2048],
                xfm[c * 128:(c + 1) * 128, 2048:4096], wbuf=f"xq{c}",
                waits=[(f"xh{c % 2}", 16 * (16 + c // 2))], tag=f"xfb{c}")

        for half in range(2):
            t0 = half * 2048
            xh = xh_up if half == 0 else xh_bf
            xk = (lambda c: f"xa{c}") if half == 0 else (lambda c: f"xq{c}")
            load_whead(wk, 0, 0, f"wk{half}.0pre")
            load_whead(wk, 1, 1, f"wk{half}.1pre")
            for h in range(H):
                slot = h % 4
                run_dq(2)
                if half == 0:
                    xlo_load(h)
                if h + 2 < H:
                    load_whead(wk, h + 2, (h + 2) % 4, f"wk{half}.{h + 2}")
                for tb in range(4):
                    bi = tb % 4
                    dst = bank(bi)
                    for c in range(NDC):
                        rb = []
                        if h == 0 and tb == 0:
                            rb.append(xk(c))
                        if c in (0, NDC - 1):
                            rb.append(f"wsb{slot}")
                        mm(dst, wslot(slot)[:, c * E:(c + 1) * E],
                           xh[:, c * 2048 + tb * 512:c * 2048 + (tb + 1) * 512],
                           start=(c == 0), stop=(c == NDC - 1),
                           wbufs=([f"ps{bi}"] if c in (0, NDC - 1) else ()), rbufs=rb,
                           tag=f"k{half}.{h}.{tb}.{c}")
                    sl = (h * 4 + tb) % 4
                    op("act", "act",
                       lambda e, d=qsb(sl), s=dst, b=bk_sb[:, h:h + 1]:
                       e.activation(out=d, in_=s, func=AF.Identity, bias=b, scale=1.0),
                       wbufs=[f"qs{sl}"], rbufs=[f"ps{bi}", "bk"], tag=f"ke{half}.{h}.{tb}")
                    dma("sp", f"qw{sl}", k_scr[h, :, t0 + tb * 512:t0 + (tb + 1) * 512],
                        qsb(sl), rbufs=[f"qs{sl}"], tag=f"kw{half}.{h}.{tb}")
            for g in range(4):
                run_dq(4)
                sr = wv.rearrange("(c p) n -> p c n", p=128)[:, :, g * 512:(g + 1) * 512]
                slab_ms = op("gp", "wsl0",
                             lambda e, d=wsb[:, 0:8192].rearrange("p (c e) -> p c e", c=NDC), s=sr:
                             e.dma_start(out=d, in_=s),
                             wbufs=["wsb0", "wsb1", "wsb2", "wsb3"],
                             tag=f"wv{half}.{g}", inc=16)
                for kt in range(16):
                    bi = kt % 4
                    dst = bank(bi)
                    for c in range(NDC):
                        rb = []
                        if g == 3 and kt == 15:
                            rb.append(xk(c))
                        if kt in (0, 15):
                            rb.append(f"wsb{c // 4}")
                        mm(dst, xh[:, c * 2048 + kt * 128:c * 2048 + (kt + 1) * 128],
                           wsb[:, c * 512:(c + 1) * 512],
                           start=(c == 0), stop=False,
                           wbufs=([f"ps{bi}"] if c == 0 else ()), rbufs=rb,
                           tag=f"v{half}.{g}.{kt}.{c}")
                    mm(dst, onesr[:], bv_sb[:, g * 512:(g + 1) * 512],
                       start=False, stop=True, wbufs=[f"ps{bi}"],
                       rbufs=["onesr", "bv"], tag=f"vb{half}.{g}.{kt}")
                    sl = kt % 4
                    op("dve", "dve",
                       lambda e, d=qsb(sl), s=dst: e.tensor_copy(out=d, in_=s),
                       wbufs=[f"qs{sl}"], rbufs=[f"ps{bi}"], tag=f"ve{half}.{g}.{kt}")
                    for hh in range(4):
                        dma("sp", f"qw{sl}",
                            v_scr[g * 4 + hh, t0 + kt * 128:t0 + (kt + 1) * 128, :],
                            qsb(sl)[:, hh * 128:(hh + 1) * 128],
                            rbufs=[f"qs{sl}"], tag=f"vw{half}.{g}.{kt}.{hh}")

        run_dq(200)
        deq_done = [("gw", p.cnt["gw"]), ("dqi0", p.cnt["dqi0"]), ("dqi1", p.cnt["dqi1"])]
        grids_done = [("gw", p.cnt["gw"])]

        k_half = lambda bb: kv_bf[:, bb * 4096:(bb + 1) * 4096]
        v_half = lambda bb: kv_bf[:, 8192 + bb * 4096:8192 + (bb + 1) * 4096]
        q_slot = lambda bb: q_bf2[:, bb * 2048:(bb + 1) * 2048]
        pslot = lambda s: p_sb[:, s * 1024:(s + 1) * 1024]
        lden = lambda cid: r2f[:, cid * 256:(cid + 1) * 256]
        linv = lambda cid: r2f[:, 512 + cid * 256:512 + (cid + 1) * 256]

        p1_done = [(f"qw{i}", p.cnt[f"qw{i}"]) for i in range(4)]
        sctr = [0]
        hstate = {}

        def emit_quad_S(h, kb, c, cid, qq, nkt):
            ss = sctr[0] % 3
            sctr[0] += 1
            sdst = bank(2 * ss, 2)
            spair = [f"ps{2 * ss}", f"ps{2 * ss + 1}"]
            for t_ in range(4):
                j = 4 * qq + t_
                li = max(0, j // 2 - 2 * c)
                n = 256 - 128 * li
                edge = (j >= 4 * c)
                rb = []
                if t_ == 3 or not hstate.get("s_seen"):
                    rb = [f"kbuf{kb}", f"qbuf{kb}"]
                    hstate["s_seen"] = True
                if edge:
                    mm(sdst[:, t_ * 256 + li * 128:t_ * 256 + li * 128 + 128],
                       idn_sb[:], mskb[:, (j & 1) * 128:((j & 1) + 1) * 128],
                       start=True, stop=False,
                       wbufs=(spair if t_ == 0 else ()),
                       rbufs=["idn", "msk"], tag=f"MB{h}.{c}.{j}")
                mm(sdst[:, t_ * 256 + li * 128:(t_ + 1) * 256],
                   k_half(kb)[:, j * 128:(j + 1) * 128],
                   q_slot(kb)[:, c * 256 + li * 128:(c + 1) * 256],
                   start=not edge, stop=True,
                   wbufs=(spair if (t_ == 3 or (t_ == 0 and not edge)) else ()),
                   rbufs=rb, tag=f"S{h}.{c}.{j}")
            op("act", "act",
               lambda e, o=pslot(ss).rearrange("p (a b) -> p a b", b=256),
                      s=sdst.rearrange("p (a b) -> p a b", b=256):
               e.activation(out=o, in_=s, func=AF.Exp, scale=ISCALE),
               wbufs=[f"pp{ss}"], rbufs=spair, tag=f"E{h}.{c}.{qq}")
            return ss

        def emit_quad_AV(h, kb, c, cid, qq, nkt, ss):
            zlb = 6 + cid
            zdst = bank(zlb)[:, 0:256]
            ldst = bank(zlb)[:, 256:512]
            last_q = (qq == nkt // 4 - 1)
            for t_ in range(4):
                j = 4 * qq + t_
                li = max(0, j // 2 - 2 * c)
                n = 256 - 128 * li
                rb = [f"pp{ss}"]
                if t_ == 3 or not hstate.get("v_seen"):
                    rb.append(f"vbuf{kb}")
                    hstate["v_seen"] = True
                if c == 7 and j == nkt - 1:
                    rb += [f"kbuf{kb}", f"qbuf{kb}", f"vbuf{kb}"]
                mm(zdst[:, li * 128:256],
                   v_half(kb)[:, j * 128:(j + 1) * 128],
                   pslot(ss)[:, t_ * 256 + li * 128:(t_ + 1) * 256],
                   start=(j == 0), stop=False,
                   wbufs=([f"ps{zlb}"] if j == 0 else ()),
                   rbufs=rb, tag=f"A{h}.{c}.{j}")
                mm(ldst[:, li * 128:256], onesm[:],
                   pslot(ss)[:, t_ * 256 + li * 128:(t_ + 1) * 256],
                   start=False, stop=(last_q and t_ == 3),
                   wbufs=([f"ps{zlb}"] if (last_q and t_ == 3) else ()),
                   rbufs=(["onesm", f"pp{ss}"] if t_ == 3 else [f"pp{ss}"]),
                   tag=f"L{h}.{c}.{j}")

        def emit_epi(h, c, cid):
            zlb = 6 + cid
            op("dve", "dve", lambda e, d=lden(cid), s=bank(zlb)[:, 256:512]:
               e.tensor_copy(out=d, in_=s),
               wbufs=[f"ldn{cid}"], rbufs=[f"ps{zlb}"], tag=f"le{h}.{c}")
            op("dve", "dve", lambda e, d=linv(cid), s=lden(cid): e.reciprocal(out=d, in_=s),
               wbufs=[f"lin{cid}"], rbufs=[f"ldn{cid}"], tag=f"lr{h}.{c}")
            op("dve", "dve",
               lambda e, d=zall[:, h * 2048 + c * 256:h * 2048 + (c + 1) * 256],
                      s=bank(zlb)[:, 0:256], i=linv(cid):
               e.tensor_tensor(out=d, in0=s, in1=i, op=ALU.mult),
               wbufs=[f"za{h}.{c}"], rbufs=[f"ps{zlb}", f"lin{cid}"],
               tag=f"ze{h}.{c}")

        def head_loads(h):
            kb = h % 2
            dma("sp", f"kv{kb}", k_half(kb), k_scr[h], wbuf=f"kbuf{kb}",
                waits=(p1_done if h < 2 else ()), tag=f"kl{h}")
            dma("sp", f"vv{kb}", v_half(kb).rearrange("p (t e) -> p t e", e=E),
                v_scr[h].rearrange("(t p) e -> p t e", p=128),
                wbuf=f"vbuf{kb}", tag=f"vl{h}")
            dma("sp", f"qq{kb}", q_slot(kb), q_scr[h], wbuf=f"qbuf{kb}", tag=f"ql{h}")

        head_loads(0)
        head_loads(1)
        for h in range(H):
            kb = h % 2
            hstate.clear()
            for duo in ((0, 1), (2, 3), (4, 5), (6, 7)):
                nq = {c: c + 1 for c in duo}
                pos = {c: 0 for c in duo}
                live = list(duo)
                sched = []
                while live:
                    for c in list(live):
                        sched.append((c, pos[c]))
                        pos[c] += 1
                        if pos[c] == nq[c]:
                            live.remove(c)
                win = deque()
                for (c, qq) in sched:
                    ss = emit_quad_S(h, kb, c, c % 2, qq, 4 * c + 4)
                    win.append((c, qq, ss))
                    if len(win) >= 3:
                        c2, q2, s2 = win.popleft()
                        emit_quad_AV(h, kb, c2, c2 % 2, q2, 4 * c2 + 4, s2)
                        if q2 == nq[c2] - 1:
                            emit_epi(h, c2, c2 % 2)
                while win:
                    c2, q2, s2 = win.popleft()
                    emit_quad_AV(h, kb, c2, c2 % 2, q2, 4 * c2 + 4, s2)
                    if q2 == nq[c2] - 1:
                        emit_epi(h, c2, c2 % 2)
            if h + 2 < H:
                head_loads(h + 2)
        if dbg:
            dma("gp", "st", dbg_out["dl"], r2f[:, 0:512], rbufs=["ldn0", "ldn1"], tag="dbg_l")

        grids_done = [("gw", p.cnt["gw"])]

        if dbg:
            dma("gp", "st", dbg_out["dq0o"], q_scr[0], tag="dbg_q")
            dma("gp", "st", dbg_out["dk0"], k_scr[0], tag="dbg_k")
            dma("gp", "st", dbg_out["dv0"], v_scr[0], tag="dbg_v")
            dma("gp", "st", dbg_out["dgup"], gup[0], waits=[("gw", p.cnt["gw"])], tag="dbg_gup")
            for hh in range(H):
                for qq in range(4):
                    pass
            dma("gp", "st", dbg_out["dz"], zall, rbufs=[f"za{hh}.{qq}" for hh in range(H) for qq in range(8)], tag="dbg_z")
        for dc in range(NDC):
            slot = dc % 2
            sr = wo.rearrange("(c p) n -> p c n", p=128)[:, :, dc * 128:(dc + 1) * 128]
            dma("gp", f"wsl{slot}", wslot(slot).rearrange("p (c e) -> p c e", c=H), sr,
                wbuf=f"wsb{slot}", tag=f"wo{dc}")
            dma("sp", "xrs", stg[:, 0:2048], xown[dc * 128:(dc + 1) * 128, :],
                wbuf="xres", waits=(deq_done if dc == 0 else ()), tag=f"xr{dc}")
            for tb in range(4):
                bi = tb % 4
                dst = bank(bi)
                for h in range(H):
                    rb = [f"za{h}.{2 * tb}", f"za{h}.{2 * tb + 1}"] if dc == 0 else []
                    if h in (0, H - 1):
                        rb.append(f"wsb{slot}")
                    mm(dst, wslot(slot)[:, h * 128:(h + 1) * 128],
                       zall[:, h * 2048 + tb * 512:h * 2048 + (tb + 1) * 512],
                       start=(h == 0), stop=(h == H - 1),
                       wbufs=([f"ps{bi}"] if h in (0, H - 1) else ()), rbufs=rb,
                       tag=f"o{dc}.{tb}.{h}")
                sl = tb % 2
                op("dve", "dve",
                   lambda e, d=qsf(sl), s=dst, b=bo_sb[:, dc:dc + 1],
                          x=stg[:, tb * 512:(tb + 1) * 512]:
                   e.scalar_tensor_tensor(out=d, in0=s, scalar=b, in1=x,
                                          op0=ALU.add, op1=ALU.add),
                   wbufs=[f"qs{sl}"], rbufs=[f"ps{bi}", "xres", "bo"], tag=f"r1{dc}.{tb}")
                wa = list(deq_done) if dc >= 10 else []
                op("act", "act",
                   lambda e, d=rbig[:, dc * 2048 + tb * 512:dc * 2048 + (tb + 1) * 512],
                          s=qsf(sl):
                   e.activation(out=d, in_=s, func=AF.Copy),
                   wbufs=[f"rb{dc}"], rbufs=[f"qs{sl}"], waits=wa,
                   tag=f"rc{dc}.{tb}")

        zall_read = [("pe", p.cnt["pe"])]

        for tb in range(4):
            sb_ = 4 + (tb % 2) * 2
            qb_ = sb_ + 1
            for dc in range(NDC):
                sl = dc % 2
                op("act", "act",
                   lambda e, d=qsb(sl), s=rbig[:, dc * 2048 + tb * 512:dc * 2048 + (tb + 1) * 512]:
                   e.activation(out=d, in_=s, func=AF.Square),
                   wbufs=[f"qs{sl}"], rbufs=[f"rb{dc}"], tag=f"sq{tb}.{dc}")
                mm(bank(sb_), onesm[:],
                   rbig[:, dc * 2048 + tb * 512:dc * 2048 + (tb + 1) * 512],
                   start=(dc == 0), stop=(dc == NDC - 1),
                   wbufs=([f"ps{sb_}"] if dc in (0, NDC - 1) else ()),
                   rbufs=["onesm", f"rb{dc}"], tag=f"sm{tb}.{dc}")
                mm(bank(qb_), onesm[:], qsb(sl),
                   start=(dc == 0), stop=(dc == NDC - 1),
                   wbufs=([f"ps{qb_}"] if dc in (0, NDC - 1) else ()),
                   rbufs=["onesm", f"qs{sl}"], tag=f"sv{tb}.{dc}")
            op("dve", "dve",
               lambda e, d=r2f[:, tb * 512:(tb + 1) * 512], s=bank(sb_):
               e.tensor_scalar_mul(out=d, in0=s, scalar1=1.0 / 2048.0),
               wbufs=[f"lnm{tb}"], rbufs=[f"ps{sb_}"], tag=f"m1.{tb}")
            op("dve", "dve",
               lambda e, d=r2f[:, 2048 + tb * 512:2048 + (tb + 1) * 512], s=bank(qb_):
               e.tensor_scalar_mul(out=d, in0=s, scalar1=1.0 / 2048.0),
               wbufs=[f"lnv{tb}"], rbufs=[f"ps{qb_}"], tag=f"v1.{tb}")
        lnm = [f"lnm{t}" for t in range(4)]
        lnv = [f"lnv{t}" for t in range(4)]
        op("dve", "dve",
           lambda e: e.tensor_tensor(out=stg[:, 0:2048], in0=r2f[:, 0:2048],
                                     in1=r2f[:, 0:2048], op=ALU.mult),
           wbufs=["xres"], rbufs=lnm, tag="mm1")
        op("dve", "dve",
           lambda e: e.tensor_sub(out=r2f[:, 2048:4096], in0=r2f[:, 2048:4096],
                                  in1=stg[:, 0:2048]),
           wbufs=lnv, rbufs=["xres"], waits=sum((B[b].rd_waits() for b in lnv), []),
           tag="var1")
        op("act", "act",
           lambda e: e.activation(out=r2f[:, 2048:4096], in_=r2f[:, 2048:4096],
                                  func=AF.Ln, bias=epsb[:]),
           wbufs=lnv, rbufs=["epsb"],
           waits=sum((B[b].wr_waits() for b in lnv), []), tag="ln1a")
        op("act", "act",
           lambda e: e.activation(out=r2f[:, 2048:4096], in_=r2f[:, 2048:4096],
                                  func=AF.Exp, scale=-0.5),
           wbufs=lnv, waits=sum((B[b].wr_waits() for b in lnv), []), tag="ln1b")
        op("dve", "dve",
           lambda e: e.scalar_tensor_tensor(out=stg[:, 0:2048], in0=r2f[:, 0:2048],
                                            scalar=-1.0, in1=r2f[:, 2048:4096],
                                            op0=ALU.mult, op1=ALU.mult),
           wbufs=["xres"], rbufs=lnm + lnv, tag="brow1")
        if dbg:
            dma("gp", "st", dbg_out["dr"], rbig, rbufs=[f"rb{d_}" for d_ in range(NDC)], tag="dbg_r")
        aslot = stg[:, 2048:3072].bitcast(BF16)[:, 0:2048]
        for dc in range(NDC):
            op("po", "po",
               lambda e, s=rbig[:, dc * 2048:(dc + 1) * 2048]:
               e.tensor_tensor(out=aslot, in0=s, in1=r2f[:, 2048:4096], op=ALU.mult),
               wbufs=["qs0", "qs1"], rbufs=[f"rb{dc}"] + lnv, tag=f"lA{dc}")
            op("po", "po",
               lambda e, d=xlnb[:, dc * 2048:(dc + 1) * 2048]:
               e.tensor_tensor(out=d, in0=aslot, in1=stg[:, 0:2048], op=ALU.add),
               wbufs=[f"xl{dc}"], rbufs=["qs0", "qs1", "xres"],
               waits=zall_read, tag=f"lB{dc}")

        upslot = lambda s: wsb[:, s * 2048:(s + 1) * 2048]
        dnslot = lambda s: wsb[:, (2 + s) * 2048:(3 + s) * 2048]
        def ln2_stat(tg, dc):
            sl = dc % 2
            op("act", "act",
               lambda e, d=qsb(sl), s=r2b[:, dc * 512:(dc + 1) * 512]:
               e.activation(out=d, in_=s, func=AF.Square),
               wbufs=[f"qs{sl}"], rbufs=[f"r2{dc}"], tag=f"s2{tg}.{dc}")
            mm(bank(6), onesm[:], r2b[:, dc * 512:(dc + 1) * 512],
               start=(dc == 0), stop=(dc == NDC - 1),
               wbufs=(["ps6"] if dc in (0, NDC - 1) else ()), rbufs=["onesm", f"r2{dc}"],
               tag=f"sm2.{tg}.{dc}")
            mm(bank(7), onesm[:], qsb(sl),
               start=(dc == 0), stop=(dc == NDC - 1),
               wbufs=(["ps7"] if dc in (0, NDC - 1) else ()), rbufs=["onesm", f"qs{sl}"],
               tag=f"sv2.{tg}.{dc}")

        for tg in range(4):
            for c in range(NDC):
                mm(bank(3)[0:R, :], aup_sb[:, c * R:(c + 1) * R],
                   xlnb[:, c * 2048 + tg * 512:c * 2048 + (tg + 1) * 512],
                   start=(c == 0), stop=(c == NDC - 1),
                   wbufs=(["ps3"] if c in (0, NDC - 1) else ()), rbufs=["aup", f"xl{c}"],
                   tag=f"tu{tg}.{c}")
            op("dve", "dve", lambda e: e.tensor_copy(out=tup_sb[:], in_=bank(3)[0:R, :]),
               wbufs=["tupb"], rbufs=["ps3"], tag=f"tue{tg}")
            for hc in range(NHC):
                s = hc % 2
                dma("sp", f"wsl{s}", upslot(s).rearrange("p (k e) -> p k e", k=NDC),
                    gup[hc], wbuf=f"wsb{s}",
                    waits=(grids_done if (tg == 0 and hc < 2) else ()),
                    tag=f"gu{tg}.{hc}")
                if hc % 8 == 0:
                    bl = (hc // 8) % 2
                    dma("sp", f"bl{bl}",
                        bupl_sb[:, bl * 1024:(bl + 1) * 1024],
                        bupl_scr[:, hc * 128:(hc + 8) * 128],
                        wbuf=f"bupl{bl}", rbufs=["buplscr"], tag=f"bl{tg}.{hc}")
                bi = hc % 2
                dst = bank(bi)
                for c in range(NDC):
                    rb = [f"wsb{s}"] if c in (0, NDC - 1) else []
                    mm(dst, upslot(s)[:, c * 128:(c + 1) * 128],
                       xlnb[:, c * 2048 + tg * 512:c * 2048 + (tg + 1) * 512],
                       start=(c == 0), stop=False,
                       wbufs=([f"ps{bi}"] if c == 0 else ()), rbufs=rb,
                       tag=f"u{tg}.{hc}.{c}")
                bl = (hc // 8) % 2
                mm(dst, bupl_sb[:, bl * 1024 + (hc % 8) * 128:bl * 1024 + (hc % 8 + 1) * 128],
                   tup_sb[:], start=False, stop=True, wbufs=[f"ps{bi}"],
                   rbufs=[f"bupl{bl}", "tupb"], tag=f"ul{tg}.{hc}")
                wx = [f"x2{hc}"] + ([f"rb{hc // 4}"] if tg == 0 else [])
                op("act", "act",
                   lambda e, d=x2b[:, hc * 512:(hc + 1) * 512], s=dst, b=bup_sb[:, hc:hc + 1]:
                   e.activation(out=d, in_=s, func=AF.Relu, bias=b, scale=1.0),
                   wbufs=wx, rbufs=[f"ps{bi}", "bup"], tag=f"re{tg}.{hc}")
                if hc >= 1:
                    h2 = hc - 1
                    mm(bank(3)[0:R, :], adn_sb[:, h2 * R:(h2 + 1) * R],
                       x2b[:, h2 * 512:(h2 + 1) * 512],
                       start=(h2 == 0), stop=False,
                       wbufs=(["ps3"] if h2 == 0 else ()), rbufs=["adn", f"x2{h2}"],
                       tag=f"t2{tg}.{h2}")
                if hc == NHC - 1:
                    mm(bank(3)[0:R, :], adn_sb[:, hc * R:(hc + 1) * R],
                       x2b[:, hc * 512:(hc + 1) * 512],
                       start=False, stop=True,
                       wbufs=["ps3"], rbufs=["adn", f"x2{hc}"],
                       tag=f"t2{tg}.{hc}")
            if dbg and tg == 0:
                dma("gp", "st", dbg_out["dxl"], xlnb,
                    rbufs=[f"xl{d_}" for d_ in range(NDC)], tag="dbg_xl")
                dma("gp", "st", dbg_out["dx2"], x2b,
                    rbufs=[f"x2{h_}" for h_ in range(NHC)], tag="dbg_x2")
            op("dve", "dve", lambda e: e.tensor_copy(out=t2_sb[:], in_=bank(3)[0:R, :]),
               wbufs=["t2b"], rbufs=["ps3"], tag=f"t2e{tg}")
            for dc in range(NDC):
                bi = 4 + (dc % 2)
                dst = bank(bi)
                for qt in range(4):
                    s = qt % 2
                    dma("sp", f"wsl{2 + s}", dnslot(s).rearrange("p (k e) -> p k e", k=16),
                        gdn[dc, qt],
                        wbuf=f"wsb{2 + s}",
                        waits=(grids_done if (tg == 0 and dc == 0 and qt < 2) else ()),
                        tag=f"gd{tg}.{dc}.{qt}")
                    for kk in range(16):
                        hc = qt * 16 + kk
                        rb = [f"x2{hc}"] if dc in (0, NDC - 1) else []
                        if kk in (0, 15):
                            rb.append(f"wsb{2 + s}")
                        mm(dst, dnslot(s)[:, kk * 128:(kk + 1) * 128],
                           x2b[:, hc * 512:(hc + 1) * 512],
                           start=(hc == 0), stop=False,
                           wbufs=([f"ps{bi}"] if hc == 0 else ()), rbufs=rb,
                           tag=f"d{tg}.{dc}.{hc}")
                mm(dst, bdnl_sb[:, dc * 128:(dc + 1) * 128], t2_sb[:],
                   start=False, stop=True, wbufs=[f"ps{bi}"],
                   rbufs=["bdnl", "t2b"], tag=f"dl{tg}.{dc}")
                op("dve", "dve",
                   lambda e, d=r2b[:, dc * 512:(dc + 1) * 512], s=dst,
                          b=bdn_sb[:, dc:dc + 1],
                          x=xlnb[:, dc * 2048 + tg * 512:dc * 2048 + (tg + 1) * 512]:
                   e.scalar_tensor_tensor(out=d, in0=s, scalar=b, in1=x,
                                          op0=ALU.add, op1=ALU.add),
                   wbufs=[f"r2{dc}"], rbufs=[f"ps{bi}", "bdn", f"xl{dc}"],
                   tag=f"r2.{tg}.{dc}")
                if dc >= 1:
                    ln2_stat(tg, dc - 1)
            ln2_stat(tg, NDC - 1)
            op("dve", "dve",
               lambda e: e.tensor_scalar_mul(out=stg[:, 0:512], in0=bank(6),
                                             scalar1=1.0 / 2048.0),
               wbufs=["l2m"], rbufs=["ps6"], tag=f"m2.{tg}")
            op("dve", "dve",
               lambda e: e.tensor_scalar_mul(out=stg[:, 512:1024], in0=bank(7),
                                             scalar1=1.0 / 2048.0),
               wbufs=["l2v"], rbufs=["ps7"], tag=f"v2.{tg}")
            op("dve", "dve",
               lambda e: e.tensor_tensor(out=stg[:, 1024:1536], in0=stg[:, 0:512],
                                         in1=stg[:, 0:512], op=ALU.mult),
               wbufs=["l2t"], rbufs=["l2m"], tag=f"mm2.{tg}")
            op("dve", "dve",
               lambda e: e.tensor_sub(out=stg[:, 512:1024], in0=stg[:, 512:1024],
                                      in1=stg[:, 1024:1536]),
               wbufs=["l2v"], rbufs=["l2t"], waits=B["l2v"].wr_waits(), tag=f"va2.{tg}")
            op("act", "act",
               lambda e: e.activation(out=stg[:, 512:1024], in_=stg[:, 512:1024],
                                      func=AF.Ln, bias=epsb[:]),
               wbufs=["l2v"], rbufs=["epsb"], waits=B["l2v"].wr_waits(), tag=f"lna2.{tg}")
            op("act", "act",
               lambda e: e.activation(out=stg[:, 512:1024], in_=stg[:, 512:1024],
                                      func=AF.Exp, scale=-0.5),
               wbufs=["l2v"], waits=B["l2v"].wr_waits(), tag=f"lnb2.{tg}")
            op("dve", "dve",
               lambda e: e.scalar_tensor_tensor(out=stg[:, 1024:1536], in0=stg[:, 0:512],
                                                scalar=-1.0, in1=stg[:, 512:1024],
                                                op0=ALU.mult, op1=ALU.mult),
               wbufs=["l2t"], rbufs=["l2m", "l2v"], tag=f"br2.{tg}")
            for dc in range(NDC):
                sl = dc % 2
                op("po", "po",
                   lambda e, d=qsf(sl), s=r2b[:, dc * 512:(dc + 1) * 512]:
                   e.tensor_tensor(out=d, in0=s, in1=stg[:, 512:1024], op=ALU.mult),
                   wbufs=[f"qs{sl}"], rbufs=[f"r2{dc}", "l2v"], tag=f"fA{tg}.{dc}")
                op("po", "po",
                   lambda e, d=qsf(sl):
                   e.tensor_add(out=d, in0=d, in1=stg[:, 1024:1536]),
                   wbufs=[f"qs{sl}"], rbufs=["l2t"], waits=B[f"qs{sl}"].wr_waits(),
                   tag=f"fB{tg}.{dc}")
                dma("gp", "st", out[dc * 128:(dc + 1) * 128, tg * 512:(tg + 1) * 512],
                    qsf(sl), rbufs=[f"qs{sl}"], tag=f"w{tg}.{dc}")

        st_total = p.cnt["st"]
        p.emit("sp", lambda e: e.nop(), [("st", st_total)], [], "fin_sp")
        p.emit("gp", lambda e: e.nop(), [("st", st_total)], [], "fin_gp")

        p.check()

        nc.all_engine_barrier()
        for s_ in sems.values():
            nc.gpsimd.sem_clear(s_)
        nc.all_engine_barrier()

        with nc.Block() as block:
            def runner(key):
                def body(eng):
                    for e_, fn, ws, incs, _tag in p.ops:
                        if e_ != key:
                            continue
                        for s, v in ws:
                            eng.wait_ge(sems[s], v)
                        inst = fn(eng)
                        for s, n in incs:
                            inst.then_inc(sems[s], n)
                return body
            block.tensor(runner("pe"))
            block.scalar(runner("act"))
            block.vector(runner("dve"))
            block.gpsimd(runner("gp"))
            block.sync(runner("sp"))

    return nc


_NC_CACHE = {}


def _get_nc():
    if "nc" not in _NC_CACHE:
        _NC_CACHE["nc"] = build_program()
    return _NC_CACHE["nc"]


def kernel(L0_attn_x, L0_attn_W_Q, L0_attn_W_K, L0_attn_W_V, L0_attn_W_O,
           L0_attn_b_Q, L0_attn_b_K, L0_attn_b_V, L0_attn_b_O,
           L0_ffn_w_up_q, L0_ffn_w_up_scale, L0_ffn_b_up,
           L0_ffn_w_up_lora_a, L0_ffn_w_up_lora_b,
           L0_ffn_w_down_q, L0_ffn_w_down_scale, L0_ffn_b_down,
           L0_ffn_w_down_lora_a, L0_ffn_w_down_lora_b):
    x = np.asarray(L0_attn_x, dtype=np.float32)
    nb = x.shape[0]
    f32 = lambda a: np.ascontiguousarray(np.asarray(a, dtype=np.float32))

    wq_l = f32(np.asarray(L0_attn_W_Q).transpose(1, 0, 2).reshape(D, H * E))
    wk_l = f32(np.asarray(L0_attn_W_K).transpose(1, 0, 2).reshape(D, H * E))
    wv_l = f32(np.asarray(L0_attn_W_V).transpose(1, 0, 2).reshape(D, H * E))
    wo_l = f32(np.asarray(L0_attn_W_O).reshape(H * E, D))
    bq_l = f32(np.asarray(L0_attn_b_Q).T)
    bk_l = f32(np.asarray(L0_attn_b_K).T)
    bv_l = f32(np.asarray(L0_attn_b_V).reshape(1, H * E))
    bo_l = f32(np.asarray(L0_attn_b_O).reshape(NDC, 128).T)
    qup_l = np.ascontiguousarray(np.asarray(L0_ffn_w_up_q, dtype=np.int32).T)
    sup_l = f32(np.asarray(L0_ffn_w_up_scale).T)
    qdn_l = np.ascontiguousarray(np.asarray(L0_ffn_w_down_q, dtype=np.int32).T)
    sdn_l = f32(np.asarray(L0_ffn_w_down_scale).T)
    sel_l = np.zeros((2, 128), np.float32)
    sel_l[0, 0:64] = 1.0
    sel_l[1, 64:128] = 1.0
    bup_l = f32(np.asarray(L0_ffn_b_up).reshape(NHC, 128).T)
    bdn_l = f32(np.asarray(L0_ffn_b_down).reshape(NDC, 128).T)
    aup_l = f32(L0_ffn_w_up_lora_a)
    bupl_l = f32(L0_ffn_w_up_lora_b)
    adn_l = f32(L0_ffn_w_down_lora_a)
    bdnl_l = f32(L0_ffn_w_down_lora_b)

    NEG = np.float32(-1e9)
    triu = (np.arange(128)[:, None] > np.arange(128)[None, :]).astype(np.float32) * NEG
    allm = np.full((128, 128), NEG, np.float32)
    zero = np.zeros((128, 128), np.float32)
    msk_p = [np.concatenate([triu, allm], axis=1),
             np.concatenate([zero, triu], axis=1)]
    idn_l = np.eye(128, dtype=np.float32)

    in_maps = []
    for c in range(8):
        b = c // 2
        par = c % 2
        xb = x[b]
        xfm_c = np.ascontiguousarray(xb.T)
        xown_c = np.ascontiguousarray(
            xb.reshape(32, 128, D)[par::2].reshape(TOWN, D).T)
        in_maps.append(dict(
            xfm=xfm_c, xown=xown_c, wq=wq_l, wk=wk_l, wv=wv_l, wo=wo_l,
            bq=bq_l, bk=bk_l, bv=bv_l, bo=bo_l, msk=np.ascontiguousarray(msk_p[par]),
            qupt=qup_l, supt=sup_l, qdnt=qdn_l, sdnt=sdn_l, sel2=sel_l, idn=idn_l,
            bup=bup_l, bdn=bdn_l, aupw=aup_l, buplw=bupl_l, adnw=adn_l,
            bdnlw=bdnl_l))

    nc = _get_nc()
    res = run_bass_kernel_spmd(nc, in_maps, core_ids=list(range(8)))

    y = np.empty((nb, T, D), dtype=np.float32)
    for c in range(8):
        b = c // 2
        par = c % 2
        o = res.results[c]["out"]
        y[b].reshape(32, 128, D)[par::2] = o.T.reshape(16, 128, D)
    return y


# version 19
# speedup vs baseline: 1.2420x; 1.0455x over previous
import numpy as np
from collections import defaultdict, deque
from contextlib import ExitStack

import concourse.bass as bass
import concourse.mybir as mybir
from concourse.bass_utils import run_bass_kernel_spmd

F32 = mybir.dt.float32
BF16 = mybir.dt.bfloat16
I32 = mybir.dt.int32
AF = mybir.ActivationFunctionType
ALU = mybir.AluOpType

D = 2048
T = 4096
TOWN = 2048
H = 16
E = 128
HF = 8192
R = 16
NDC = D // 128
NHC = HF // 128
ISCALE = float(1.0 / np.sqrt(np.float32(128.0)))
LNEPS = 1e-5

ENGS = ("pe", "act", "dve", "sp", "gp")
STREAM = {"pe": "pe", "act": "act", "ald": "act", "dve": "dve",
          "sp": "sp", "gp": "gp", "po": "gp"}


class Prog:
    def __init__(self):
        self.ops = []
        self.cnt = defaultdict(int)

    def emit(self, stream, fn, waits=(), incs=(), tag=""):
        w = {}
        for s, v in waits:
            if v > 0:
                w[s] = max(w.get(s, 0), v)
        ws = sorted(w.items())
        self.ops.append((STREAM[stream], fn, ws, list(incs), tag))
        out = []
        for s, n in incs:
            self.cnt[s] += n
            out.append((s, self.cnt[s]))
        return out

    def check(self):
        qs = {e: deque() for e in ENGS}
        for opr in self.ops:
            qs[opr[0]].append(opr)
        cnt = defaultdict(int)
        total = len(self.ops)
        done = 0
        while done < total:
            prog = False
            for e in ENGS:
                while qs[e]:
                    _, fn, ws, incs, tag = qs[e][0]
                    if all(cnt[s] >= v for s, v in ws):
                        for s, n in incs:
                            cnt[s] += n
                        qs[e].popleft()
                        done += 1
                        prog = True
                    else:
                        break
            if not prog:
                blocked = {e: (qs[e][0][4], [(s, v, cnt[s]) for s, v in qs[e][0][2]
                                             if cnt[s] < v])
                           for e in ENGS if qs[e]}
                raise RuntimeError(f"schedule deadlock: {blocked}")


class Buf:
    __slots__ = ("ready", "readers")

    def __init__(self):
        self.ready = {}
        self.readers = {}

    @staticmethod
    def _merge(d, ms):
        for s, v in ms:
            d[s] = max(d.get(s, 0), v)

    def wr_waits(self):
        return list(self.readers.items()) + list(self.ready.items())

    def rd_waits(self):
        return list(self.ready.items())

    def set_written(self, ms):
        self.ready = dict(ms)
        self.readers = {}

    def add_reader(self, ms):
        self._merge(self.readers, ms)


def build_program(dbg=False):
    nc = bass.Bass("TRN2", target_bir_lowering=False, debug=False)
    dri = lambda name, shape, dt=F32: nc.dram_tensor(name, shape, dt, kind="ExternalInput").ap()

    xfm = dri("xfm", [D, T])
    xown = dri("xown", [D, TOWN])
    wq = dri("wq", [D, H * E])
    wk = dri("wk", [D, H * E])
    wv = dri("wv", [D, H * E])
    wo = dri("wo", [H * E, D])
    bq = dri("bq", [E, H])
    bk = dri("bk", [E, H])
    bv = dri("bv", [1, H * E])
    bo = dri("bo", [128, NDC])
    msk = dri("msk", [128, 256])
    qupt = dri("qupt", [D, HF], I32)
    supt = dri("supt", [32, HF])
    qdnt = dri("qdnt", [HF, D], I32)
    sdnt = dri("sdnt", [128, D])
    sel2 = dri("sel2", [2, 128])
    idn = dri("idn", [128, 128])
    bup = dri("bup", [128, NHC])
    bdn = dri("bdn", [128, NDC])
    aupw = dri("aupw", [D, R])
    buplw = dri("buplw", [R, HF])
    adnw = dri("adnw", [HF, R])
    bdnlw = dri("bdnlw", [R, D])

    out = nc.dram_tensor("out", [D, TOWN], F32, kind="ExternalOutput").ap()

    dbg_out = {}
    if dbg:
        for nm, shape, dt in [("dq0o", [E, TOWN], BF16), ("dk0", [E, T], BF16),
                              ("dv0", [T, E], BF16), ("dz", [128, 32768], BF16),
                              ("dr", [128, 32768], BF16), ("dxl", [128, 32768], BF16),
                              ("dx2", [128, 32768], BF16), ("dup", [HF, 128], BF16),
                              ("dgup", [128, NDC, 128], BF16), ("dl", [128, 512], F32)]:
            dbg_out[nm] = nc.dram_tensor(nm, shape, dt, kind="ExternalOutput").ap()
    q_scr = nc.dram_tensor("q_scr", [H, E, TOWN], BF16, kind="Internal").ap()
    k_scr = nc.dram_tensor("k_scr", [H, E, T], BF16, kind="Internal").ap()
    v_scr = nc.dram_tensor("v_scr", [H, T, E], BF16, kind="Internal").ap()
    bupl_scr = nc.dram_tensor("bupl_scr", [R, HF], BF16, kind="Internal").ap()
    gup = nc.dram_tensor("gup", [NHC, 128, NDC, 128], BF16, kind="Internal").ap()
    gdn = nc.dram_tensor("gdn", [NDC, 4, 128, 16, 128], BF16, kind="Internal").ap()

    p = Prog()
    B = defaultdict(Buf)

    with ExitStack() as ctx:
        en = ctx.enter_context
        arena = en(nc.sbuf_tensor("arena", [128, 131072], mybir.dt.uint8))
        a_bf = arena[:].bitcast(BF16)
        xh_bf = a_bf[:, 0:32768]
        zall = a_bf[:, 0:32768]
        xlnb = a_bf[:, 0:32768]
        rbig = a_bf[:, 32768:65536]
        x2b = a_bf[:, 32768:65536]
        p_sb = a_bf[:, 32768:36864]
        kq0 = 36864
        kv_bf = a_bf[:, kq0:kq0 + 16384]
        q0 = kq0 + 16384
        q_bf2 = a_bf[:, q0:q0 + 4096]
        xh_up = a_bf[:, 32768:65536]

        wsb = en(nc.sbuf_tensor("wsb", [128, 8192], BF16))
        stg = en(nc.sbuf_tensor("stg", [128, 3072], F32))
        r2b = en(nc.sbuf_tensor("r2b", [128, 8192], BF16))
        r2f = r2b[:].bitcast(F32)
        dq_in_s = [r2f[:, 2048:3072].bitcast(BF16)[:, 0:2048],
                   r2f[:, 3072:4096].bitcast(BF16)[:, 0:2048]]
        dq_out_s = [stg[:, 0:1024].bitcast(BF16)[:, 0:2048],
                    stg[:, 1024:2048].bitcast(BF16)[:, 0:2048]]
        mskb = en(nc.sbuf_tensor("mskb", [128, 256], BF16))

        onesm = en(nc.sbuf_tensor("onesm", [128, 128], BF16))
        onesr = en(nc.sbuf_tensor("onesr", [1, 128], BF16))
        bq_sb = en(nc.sbuf_tensor("bq_sb", [128, H], F32))
        bk_sb = en(nc.sbuf_tensor("bk_sb", [128, H], F32))
        bv_sb = en(nc.sbuf_tensor("bv_sb", [1, H * E], BF16))
        bo_sb = en(nc.sbuf_tensor("bo_sb", [128, NDC], F32))
        bup_sb = en(nc.sbuf_tensor("bup_sb", [128, NHC], F32))
        bdn_sb = en(nc.sbuf_tensor("bdn_sb", [128, NDC], F32))
        aup_sb = en(nc.sbuf_tensor("aup_sb", [128, NDC * R], BF16))
        adn_sb = en(nc.sbuf_tensor("adn_sb", [128, NHC * R], BF16))
        bupl_sb = en(nc.sbuf_tensor("bupl_sb", [R, 2048], BF16))
        bdnl_sb = en(nc.sbuf_tensor("bdnl_sb", [R, D], BF16))
        tup_sb = en(nc.sbuf_tensor("tup_sb", [R, 512], BF16))
        t2_sb = en(nc.sbuf_tensor("t2_sb", [R, 512], BF16))
        srow_sb = en(nc.sbuf_tensor("srow_sb", [2, 4096], BF16))
        sel_sb = en(nc.sbuf_tensor("sel_sb", [2, 128], BF16))
        idn_sb = en(nc.sbuf_tensor("idn_sb", [128, 128], BF16))
        epsb = en(nc.sbuf_tensor("epsb", [128, 1], F32))

        psAB = en(nc.psum_tensor("psAB", [128, 2048], F32))
        psCD = en(nc.psum_tensor("psCD", [128, 1024], F32))
        psEF = en(nc.psum_tensor("psEF", [128, 1024], F32))

        def bank(i, n=1):
            if i + n <= 4:
                return psAB[:, i * 512:(i + n) * 512]
            if 4 <= i and i + n <= 6:
                return psCD[:, (i - 4) * 512:(i - 4 + n) * 512]
            assert 6 <= i and i + n <= 8
            return psEF[:, (i - 6) * 512:(i - 6 + n) * 512]

        sems = {}
        for s in ("cst", "cstg", "xh0", "xh1", "wsl0", "wsl1", "wsl2", "wsl3",
                  "qw0", "qw1", "qw2", "qw3", "dqi0", "dqi1", "srw0", "srw1", "gw", "xrs",
                  "kv0", "kv1", "vv0", "vv1", "qq0", "qq1", "bl0", "bl1",
                  "st", "pe", "act", "dve", "po"):
            sems[s] = en(nc.semaphore(s))

        def op(stream, sem, fn, wbufs=(), rbufs=(), waits=(), tag="", inc=1):
            ws = list(waits)
            for wb in wbufs:
                ws += B[wb].wr_waits()
            for rb in rbufs:
                ws += B[rb].rd_waits()
            ms = p.emit(stream, fn, ws, [(sem, inc)], tag)
            for wb in wbufs:
                B[wb].set_written(ms)
            for rb in rbufs:
                B[rb].add_reader(ms)
            return ms

        def dma(stream, sem, dst, src, wbuf=None, rbufs=(), waits=(), tag="", tr=False):
            fn = (lambda e, d=dst, s=src: e.dma_start(out=d, in_=s, transpose=True)) if tr \
                else (lambda e, d=dst, s=src: e.dma_start(out=d, in_=s))
            return op(stream, sem, fn, wbufs=([wbuf] if wbuf else ()), rbufs=rbufs,
                      waits=waits, tag=tag, inc=16)

        def mm(dst, lhsT, rhs, start, stop, wbufs=(), rbufs=(), waits=(), tag=""):
            return op("pe", "pe",
                      lambda e, d=dst, l=lhsT, r=rhs, st=start, sp=stop:
                      e.matmul(d, l, r, start=st, stop=sp),
                      wbufs, rbufs, waits, tag)

        def batch_seal(sem, bufs):
            tot = p.cnt[sem]
            for b_ in bufs:
                B[b_].ready = {sem: tot}
        wslot = lambda k: wsb[:, k * 2048:(k + 1) * 2048]
        def qsf(k):
            if k < 2:
                return stg[:, 2048 + k * 512:2048 + (k + 1) * 512]
            return r2f[:, 1024 + (k - 2) * 512:1024 + (k - 1) * 512]
        qsb = lambda k: qsf(k).bitcast(BF16)[:, 0:512]

        op("dve", "dve", lambda e: e.memset(onesm[:], 1.0), wbufs=["onesm"], tag="ones")
        op("dve", "dve", lambda e: e.memset(onesr[:], 1.0), wbufs=["onesr"], tag="onesr")
        op("dve", "dve", lambda e: e.memset(epsb[:], LNEPS), wbufs=["epsb"], tag="eps")
        dma("sp", "cst", bq_sb[:], bq, wbuf="bq", tag="bq")
        dma("sp", "cst", bk_sb[:], bk, wbuf="bk", tag="bk")
        dma("sp", "cst", bo_sb[:], bo, wbuf="bo", tag="bo")
        dma("sp", "cst", bup_sb[:], bup, wbuf="bup", tag="bup")
        dma("sp", "cst", bdn_sb[:], bdn, wbuf="bdn", tag="bdn")
        dma("gp", "cstg", bv_sb[:], bv, wbuf="bv", tag="bv")
        dma("gp", "cstg", mskb[:], msk, wbuf="msk", tag="msk")
        dma("gp", "cstg", aup_sb[:].rearrange("p (c r) -> p c r", c=NDC),
            aupw.rearrange("(c p) r -> p c r", p=128), wbuf="aup", tag="aup")
        dma("gp", "cstg", adn_sb[:].rearrange("p (c r) -> p c r", c=NHC),
            adnw.rearrange("(c p) r -> p c r", p=128), wbuf="adn", tag="adn")
        dma("gp", "cstg", bdnl_sb[:], bdnlw, wbuf="bdnl", tag="bdnl")
        dma("gp", "cstg", bupl_scr, buplw, wbuf="buplscr", tag="buplscr")
        batch_seal("cst", ["bq", "bk", "bo", "bup", "bdn"])
        batch_seal("cstg", ["bv", "msk", "aup", "adn", "bdnl"])

        def load_whead(src, h, slot, tag):
            sr = src.rearrange("(c p) n -> p c n", p=128)[:, :, h * E:(h + 1) * E]
            return dma("gp", f"wsl{slot}", wslot(slot).rearrange("p (c e) -> p c e", c=NDC),
                       sr, wbuf=f"wsb{slot}", tag=tag)

        dma("gp", "cstg", sel_sb[:], sel2, wbuf="sel", tag="sel2")
        dma("gp", "cstg", idn_sb[:], idn, wbuf="idn", tag="idn")
        dq_jobs = [("u", dc, hq) for dc in range(NDC) for hq in range(4)] + \
                  [("d", hc, 0) for hc in range(NHC)]
        dq_state = {"lc": 0, "cc": 0}

        def _dq_aps(i):
            kind, a1, a2 = dq_jobs[i]
            if kind == "u":
                dc, hq = a1, a2
                return (qupt[dc * 128:(dc + 1) * 128, hq * 2048:(hq + 1) * 2048],
                        supt[2 * dc:2 * dc + 2, hq * 2048:(hq + 1) * 2048],
                        gup[hq * 16:(hq + 1) * 16, :, dc, :], f"u{dc}.{hq}")
            hc = a1
            return (qdnt[hc * 128:(hc + 1) * 128, :],
                    sdnt[2 * hc:2 * hc + 2, :],
                    gdn[:, hc // 16, :, hc % 16, :], f"d{hc}")

        def _dq_load(i):
            s = i % 2
            qa, sa, _, tg = _dq_aps(i)
            dma("gp", f"dqi{s}", dq_in_s[s], qa, wbuf=f"dqin{s}", tag=f"{tg}i")
            dma("gp", f"srw{s}", srow_sb[:, s * 2048:(s + 1) * 2048], sa,
                wbuf=f"srow{s}", tag=f"{tg}s")

        def _dq_compute(i):
            s = i % 2
            _, _, ga, tg = _dq_aps(i)
            for pc in range(4):
                bk_i = 4 + pc
                mm(bank(bk_i), sel_sb[:],
                   srow_sb[:, s * 2048 + pc * 512:s * 2048 + (pc + 1) * 512],
                   start=True, stop=True,
                   wbufs=[f"ps{bk_i}"],
                   rbufs=(["sel", f"srow{s}"] if pc in (0, 3) else [f"srow{s}"]),
                   tag=f"{tg}.b{pc}")
                op("dve", "dve",
                   lambda e, o=dq_out_s[s][:, pc * 512:(pc + 1) * 512],
                          ii=dq_in_s[s][:, pc * 512:(pc + 1) * 512],
                          sr=bank(bk_i):
                   e.scalar_tensor_tensor(out=o, in0=ii, scalar=-7.5, in1=sr,
                                          op0=ALU.add, op1=ALU.mult),
                   wbufs=([f"dqout{s}"] if pc in (0, 3) else ()),
                   rbufs=[f"dqin{s}" if pc == 0 else f"dqout{s}", f"ps{bk_i}"],
                   tag=f"{tg}.q{pc}")
            dma("sp", "gw", ga.rearrange("k p e -> p k e"),
                dq_out_s[s].rearrange("p (k e) -> p k e", k=16),
                rbufs=[f"dqout{s}"], tag=f"{tg}w")

        def run_dq(n):
            for _ in range(n):
                if dq_state["lc"] < min(dq_state["cc"] + 2, len(dq_jobs)):
                    _dq_load(dq_state["lc"])
                    dq_state["lc"] += 1
                if dq_state["cc"] < dq_state["lc"]:
                    _dq_compute(dq_state["cc"])
                    dq_state["cc"] += 1

        load_whead(wq, 0, 0, "wq0pre")
        load_whead(wq, 1, 1, "wq1pre")
        for c in range(NDC):
            dma("gp", f"xh{c % 2}", xh_bf[:, c * 2048:(c + 1) * 2048],
                xown[c * 128:(c + 1) * 128, :], wbuf=f"xq{c}",
                waits=[(f"xh{c % 2}", 16 * (c // 2))], tag=f"xo{c}")

        def xup_load(c):
            dma("gp", f"xh{c % 2}", xh_up[:, c * 2048:(c + 1) * 2048],
                xfm[c * 128:(c + 1) * 128, 0:2048], wbuf=f"xa{c}",
                waits=[(f"xh{c % 2}", 16 * (8 + c // 2))], tag=f"xfa{c}")

        for h in range(H):
            slot = h % 4
            if h + 2 < H:
                load_whead(wq, h + 2, (h + 2) % 4, f"wq{h + 2}")
            for qb in range(4):
                bi = (h * 4 + qb) % 4
                dst = bank(bi)
                for c in range(NDC):
                    rb = []
                    if h == 0 and qb == 0:
                        rb.append(f"xq{c}")
                    if h == H - 1 and qb == 3:
                        rb.append(f"xq{c}")
                    if c in (0, NDC - 1):
                        rb.append(f"wsb{slot}")
                    mm(dst, wslot(slot)[:, c * E:(c + 1) * E],
                       xh_bf[:, c * 2048 + qb * 512:c * 2048 + (qb + 1) * 512],
                       start=(c == 0), stop=(c == NDC - 1),
                       wbufs=([f"ps{bi}"] if c in (0, NDC - 1) else ()), rbufs=rb,
                       tag=f"q{h}.{qb}.{c}")
                sl = (h * 4 + qb) % 4
                op("act", "act",
                   lambda e, d=qsb(sl), s=dst, b=bq_sb[:, h:h + 1]:
                   e.activation(out=d, in_=s, func=AF.Identity, bias=b, scale=1.0),
                   wbufs=[f"qs{sl}"], rbufs=[f"ps{bi}", "bq"], tag=f"qe{h}.{qb}")
                dma("sp", f"qw{sl}", q_scr[h, :, qb * 512:(qb + 1) * 512], qsb(sl),
                    rbufs=[f"qs{sl}"], tag=f"qw{h}.{qb}")
            run_dq(2)
            xup_load(h)

        def xlo_load(c):
            dma("gp", f"xh{c % 2}", xh_bf[:, c * 2048:(c + 1) * 2048],
                xfm[c * 128:(c + 1) * 128, 2048:4096], wbuf=f"xq{c}",
                waits=[(f"xh{c % 2}", 16 * (16 + c // 2))], tag=f"xfb{c}")

        for half in range(2):
            t0 = half * 2048
            xh = xh_up if half == 0 else xh_bf
            xk = (lambda c: f"xa{c}") if half == 0 else (lambda c: f"xq{c}")
            load_whead(wk, 0, 0, f"wk{half}.0pre")
            load_whead(wk, 1, 1, f"wk{half}.1pre")
            for h in range(H):
                slot = h % 4
                run_dq(2)
                if half == 0:
                    xlo_load(h)
                if h + 2 < H:
                    load_whead(wk, h + 2, (h + 2) % 4, f"wk{half}.{h + 2}")
                for tb in range(4):
                    bi = tb % 4
                    dst = bank(bi)
                    for c in range(NDC):
                        rb = []
                        if h == 0 and tb == 0:
                            rb.append(xk(c))
                        if c in (0, NDC - 1):
                            rb.append(f"wsb{slot}")
                        mm(dst, wslot(slot)[:, c * E:(c + 1) * E],
                           xh[:, c * 2048 + tb * 512:c * 2048 + (tb + 1) * 512],
                           start=(c == 0), stop=(c == NDC - 1),
                           wbufs=([f"ps{bi}"] if c in (0, NDC - 1) else ()), rbufs=rb,
                           tag=f"k{half}.{h}.{tb}.{c}")
                    sl = (h * 4 + tb) % 4
                    op("act", "act",
                       lambda e, d=qsb(sl), s=dst, b=bk_sb[:, h:h + 1]:
                       e.activation(out=d, in_=s, func=AF.Identity, bias=b, scale=1.0),
                       wbufs=[f"qs{sl}"], rbufs=[f"ps{bi}", "bk"], tag=f"ke{half}.{h}.{tb}")
                    dma("sp", f"qw{sl}", k_scr[h, :, t0 + tb * 512:t0 + (tb + 1) * 512],
                        qsb(sl), rbufs=[f"qs{sl}"], tag=f"kw{half}.{h}.{tb}")
            for g in range(4):
                run_dq(4)
                sr = wv.rearrange("(c p) n -> p c n", p=128)[:, :, g * 512:(g + 1) * 512]
                slab_ms = op("gp", "wsl0",
                             lambda e, d=wsb[:, 0:8192].rearrange("p (c e) -> p c e", c=NDC), s=sr:
                             e.dma_start(out=d, in_=s),
                             wbufs=["wsb0", "wsb1", "wsb2", "wsb3"],
                             tag=f"wv{half}.{g}", inc=16)
                for kt in range(16):
                    bi = kt % 4
                    dst = bank(bi)
                    for c in range(NDC):
                        rb = []
                        if g == 3 and kt == 15:
                            rb.append(xk(c))
                        if kt in (0, 15):
                            rb.append(f"wsb{c // 4}")
                        mm(dst, xh[:, c * 2048 + kt * 128:c * 2048 + (kt + 1) * 128],
                           wsb[:, c * 512:(c + 1) * 512],
                           start=(c == 0), stop=False,
                           wbufs=([f"ps{bi}"] if c == 0 else ()), rbufs=rb,
                           tag=f"v{half}.{g}.{kt}.{c}")
                    mm(dst, onesr[:], bv_sb[:, g * 512:(g + 1) * 512],
                       start=False, stop=True, wbufs=[f"ps{bi}"],
                       rbufs=["onesr", "bv"], tag=f"vb{half}.{g}.{kt}")
                    sl = kt % 4
                    op("dve", "dve",
                       lambda e, d=qsb(sl), s=dst: e.tensor_copy(out=d, in_=s),
                       wbufs=[f"qs{sl}"], rbufs=[f"ps{bi}"], tag=f"ve{half}.{g}.{kt}")
                    for hh in range(4):
                        dma("sp", f"qw{sl}",
                            v_scr[g * 4 + hh, t0 + kt * 128:t0 + (kt + 1) * 128, :],
                            qsb(sl)[:, hh * 128:(hh + 1) * 128],
                            rbufs=[f"qs{sl}"], tag=f"vw{half}.{g}.{kt}.{hh}")

        run_dq(200)
        deq_done = [("gw", p.cnt["gw"]), ("dqi0", p.cnt["dqi0"]), ("dqi1", p.cnt["dqi1"])]
        grids_done = [("gw", p.cnt["gw"])]

        k_half = lambda bb: kv_bf[:, bb * 4096:(bb + 1) * 4096]
        v_half = lambda bb: kv_bf[:, 8192 + bb * 4096:8192 + (bb + 1) * 4096]
        q_slot = lambda bb: q_bf2[:, bb * 2048:(bb + 1) * 2048]
        pslot = lambda s: p_sb[:, s * 1024:(s + 1) * 1024]
        ltmp = lambda i: p_sb[:, 3072 + i * 256:3072 + (i + 1) * 256]
        lden = lambda cid: r2f[:, cid * 256:(cid + 1) * 256]
        linv = lambda cid: r2f[:, 512 + cid * 256:512 + (cid + 1) * 256]

        p1_done = [(f"qw{i}", p.cnt[f"qw{i}"]) for i in range(4)]
        sctr = [0]
        avctr = [0]
        hstate = {}

        def emit_quad_S(h, kb, c, cid, qq, nkt):
            ss = sctr[0] % 3
            sctr[0] += 1
            sdst = bank(2 * ss, 2)
            spair = [f"ps{2 * ss}", f"ps{2 * ss + 1}"]
            for t_ in range(4):
                j = 4 * qq + t_
                li = max(0, j // 2 - 2 * c)
                n = 256 - 128 * li
                edge = (j >= 4 * c)
                rb = []
                if t_ == 3 or not hstate.get("s_seen"):
                    rb = [f"kbuf{kb}", f"qbuf{kb}"]
                    hstate["s_seen"] = True
                if edge:
                    mm(sdst[:, t_ * 256 + li * 128:t_ * 256 + li * 128 + 128],
                       idn_sb[:], mskb[:, (j & 1) * 128:((j & 1) + 1) * 128],
                       start=True, stop=False,
                       wbufs=(spair if t_ == 0 else ()),
                       rbufs=["idn", "msk"], tag=f"MB{h}.{c}.{j}")
                mm(sdst[:, t_ * 256 + li * 128:(t_ + 1) * 256],
                   k_half(kb)[:, j * 128:(j + 1) * 128],
                   q_slot(kb)[:, c * 256 + li * 128:(c + 1) * 256],
                   start=not edge, stop=True,
                   wbufs=(spair if (t_ == 3 or (t_ == 0 and not edge)) else ()),
                   rbufs=rb, tag=f"S{h}.{c}.{j}")
            op("act", "act",
               lambda e, o=pslot(ss).rearrange("p (a b) -> p a b", b=256),
                      s=sdst.rearrange("p (a b) -> p a b", b=256):
               e.activation(out=o, in_=s, func=AF.Exp, scale=ISCALE),
               wbufs=[f"pp{ss}"], rbufs=spair, tag=f"E{h}.{c}.{qq}")
            return ss

        def emit_quad_AV(h, kb, c, cid, qq, nkt, ss):
            zlb = 6 + cid
            zdst = bank(zlb)[:, 0:256]
            ldst = bank(zlb)[:, 256:512]
            last_q = (qq == nkt // 4 - 1)
            for t_ in range(4):
                j = 4 * qq + t_
                li = max(0, j // 2 - 2 * c)
                n = 256 - 128 * li
                rb = [f"pp{ss}"]
                if t_ == 3 or not hstate.get("v_seen"):
                    rb.append(f"vbuf{kb}")
                    hstate["v_seen"] = True
                if c == 7 and j == nkt - 1:
                    rb += [f"kbuf{kb}", f"qbuf{kb}", f"vbuf{kb}"]
                mm(zdst[:, li * 128:256],
                   v_half(kb)[:, j * 128:(j + 1) * 128],
                   pslot(ss)[:, t_ * 256 + li * 128:(t_ + 1) * 256],
                   start=(j == 0), stop=False,
                   wbufs=([f"ps{zlb}"] if j == 0 else ()),
                   rbufs=rb, tag=f"A{h}.{c}.{j}")
            av = avctr[0] % 2
            avctr[0] += 1
            tA = ltmp(2 * av)
            tB = ltmp(2 * av + 1)
            if not last_q:
                op("dve", "dve",
                   lambda e, d=tA, a=pslot(ss)[:, 0:256], b=pslot(ss)[:, 256:512]:
                   e.tensor_add(out=d, in0=a, in1=b),
                   wbufs=["ltA%d" % av], rbufs=[f"pp{ss}"], tag=f"p1{h}.{c}.{qq}")
                op("dve", "dve",
                   lambda e, d=tB, a=pslot(ss)[:, 512:768], b=pslot(ss)[:, 768:1024]:
                   e.tensor_add(out=d, in0=a, in1=b),
                   wbufs=["ltB%d" % av], rbufs=[f"pp{ss}"], tag=f"p2{h}.{c}.{qq}")
                op("dve", "dve",
                   lambda e, d=tA, a=tA, b=tB:
                   e.tensor_add(out=d, in0=a, in1=b),
                   wbufs=["ltA%d" % av], rbufs=["ltB%d" % av],
                   tag=f"p3{h}.{c}.{qq}")
                mm(ldst[:, 0:256], onesm[:], tA,
                   start=False, stop=False,
                   rbufs=["onesm", "ltA%d" % av], tag=f"L{h}.{c}.{qq}")
            else:
                op("dve", "dve",
                   lambda e, d=tA, a=pslot(ss)[:, 0:256], b=pslot(ss)[:, 256:512]:
                   e.tensor_add(out=d, in0=a, in1=b),
                   wbufs=["ltA%d" % av], rbufs=[f"pp{ss}"], tag=f"p1{h}.{c}.{qq}")
                op("dve", "dve",
                   lambda e, d=tB[:, 0:128], a=pslot(ss)[:, 640:768], b=pslot(ss)[:, 896:1024]:
                   e.tensor_add(out=d, in0=a, in1=b),
                   wbufs=["ltB%d" % av], rbufs=[f"pp{ss}"], tag=f"p2{h}.{c}.{qq}")
                mm(ldst[:, 0:256], onesm[:], tA,
                   start=False, stop=False,
                   rbufs=["onesm", "ltA%d" % av], tag=f"La{h}.{c}.{qq}")
                mm(ldst[:, 128:256], onesm[:], tB[:, 0:128],
                   start=False, stop=True,
                   wbufs=[f"ps{zlb}"],
                   rbufs=["onesm", "ltB%d" % av], tag=f"Lb{h}.{c}.{qq}")


        def emit_epi(h, c, cid):
            zlb = 6 + cid
            op("dve", "dve", lambda e, d=lden(cid), s=bank(zlb)[:, 256:512]:
               e.tensor_copy(out=d, in_=s),
               wbufs=[f"ldn{cid}"], rbufs=[f"ps{zlb}"], tag=f"le{h}.{c}")
            op("dve", "dve", lambda e, d=linv(cid), s=lden(cid): e.reciprocal(out=d, in_=s),
               wbufs=[f"lin{cid}"], rbufs=[f"ldn{cid}"], tag=f"lr{h}.{c}")
            op("dve", "dve",
               lambda e, d=zall[:, h * 2048 + c * 256:h * 2048 + (c + 1) * 256],
                      s=bank(zlb)[:, 0:256], i=linv(cid):
               e.tensor_tensor(out=d, in0=s, in1=i, op=ALU.mult),
               wbufs=[f"za{h}.{c}"], rbufs=[f"ps{zlb}", f"lin{cid}"],
               tag=f"ze{h}.{c}")

        def head_loads(h):
            kb = h % 2
            dma("sp", f"kv{kb}", k_half(kb), k_scr[h], wbuf=f"kbuf{kb}",
                waits=(p1_done if h < 2 else ()), tag=f"kl{h}")
            dma("sp", f"vv{kb}", v_half(kb).rearrange("p (t e) -> p t e", e=E),
                v_scr[h].rearrange("(t p) e -> p t e", p=128),
                wbuf=f"vbuf{kb}", tag=f"vl{h}")
            dma("sp", f"qq{kb}", q_slot(kb), q_scr[h], wbuf=f"qbuf{kb}", tag=f"ql{h}")

        head_loads(0)
        head_loads(1)
        for h in range(H):
            kb = h % 2
            hstate.clear()
            for duo in ((0, 1), (2, 3), (4, 5), (6, 7)):
                nq = {c: c + 1 for c in duo}
                pos = {c: 0 for c in duo}
                live = list(duo)
                sched = []
                while live:
                    for c in list(live):
                        sched.append((c, pos[c]))
                        pos[c] += 1
                        if pos[c] == nq[c]:
                            live.remove(c)
                win = deque()
                for (c, qq) in sched:
                    ss = emit_quad_S(h, kb, c, c % 2, qq, 4 * c + 4)
                    win.append((c, qq, ss))
                    if len(win) >= 3:
                        c2, q2, s2 = win.popleft()
                        emit_quad_AV(h, kb, c2, c2 % 2, q2, 4 * c2 + 4, s2)
                        if q2 == nq[c2] - 1:
                            emit_epi(h, c2, c2 % 2)
                while win:
                    c2, q2, s2 = win.popleft()
                    emit_quad_AV(h, kb, c2, c2 % 2, q2, 4 * c2 + 4, s2)
                    if q2 == nq[c2] - 1:
                        emit_epi(h, c2, c2 % 2)
            if h + 2 < H:
                head_loads(h + 2)
        if dbg:
            dma("gp", "st", dbg_out["dl"], r2f[:, 0:512], rbufs=["ldn0", "ldn1"], tag="dbg_l")

        grids_done = [("gw", p.cnt["gw"])]

        if dbg:
            dma("gp", "st", dbg_out["dq0o"], q_scr[0], tag="dbg_q")
            dma("gp", "st", dbg_out["dk0"], k_scr[0], tag="dbg_k")
            dma("gp", "st", dbg_out["dv0"], v_scr[0], tag="dbg_v")
            dma("gp", "st", dbg_out["dgup"], gup[0], waits=[("gw", p.cnt["gw"])], tag="dbg_gup")
            for hh in range(H):
                for qq in range(4):
                    pass
            dma("gp", "st", dbg_out["dz"], zall, rbufs=[f"za{hh}.{qq}" for hh in range(H) for qq in range(8)], tag="dbg_z")
        for dc in range(NDC):
            slot = dc % 2
            sr = wo.rearrange("(c p) n -> p c n", p=128)[:, :, dc * 128:(dc + 1) * 128]
            dma("gp", f"wsl{slot}", wslot(slot).rearrange("p (c e) -> p c e", c=H), sr,
                wbuf=f"wsb{slot}", tag=f"wo{dc}")
            dma("sp", "xrs", stg[:, 0:2048], xown[dc * 128:(dc + 1) * 128, :],
                wbuf="xres", waits=(deq_done if dc == 0 else ()), tag=f"xr{dc}")
            for tb in range(4):
                bi = tb % 4
                dst = bank(bi)
                for h in range(H):
                    rb = [f"za{h}.{2 * tb}", f"za{h}.{2 * tb + 1}"] if dc == 0 else []
                    if h in (0, H - 1):
                        rb.append(f"wsb{slot}")
                    mm(dst, wslot(slot)[:, h * 128:(h + 1) * 128],
                       zall[:, h * 2048 + tb * 512:h * 2048 + (tb + 1) * 512],
                       start=(h == 0), stop=(h == H - 1),
                       wbufs=([f"ps{bi}"] if h in (0, H - 1) else ()), rbufs=rb,
                       tag=f"o{dc}.{tb}.{h}")
                sl = tb % 2
                op("dve", "dve",
                   lambda e, d=qsf(sl), s=dst, b=bo_sb[:, dc:dc + 1],
                          x=stg[:, tb * 512:(tb + 1) * 512]:
                   e.scalar_tensor_tensor(out=d, in0=s, scalar=b, in1=x,
                                          op0=ALU.add, op1=ALU.add),
                   wbufs=[f"qs{sl}"], rbufs=[f"ps{bi}", "xres", "bo"], tag=f"r1{dc}.{tb}")
                wa = list(deq_done) if dc >= 10 else []
                op("act", "act",
                   lambda e, d=rbig[:, dc * 2048 + tb * 512:dc * 2048 + (tb + 1) * 512],
                          s=qsf(sl):
                   e.activation(out=d, in_=s, func=AF.Copy),
                   wbufs=[f"rb{dc}"], rbufs=[f"qs{sl}"], waits=wa,
                   tag=f"rc{dc}.{tb}")

        zall_read = [("pe", p.cnt["pe"])]

        for tb in range(4):
            sb_ = 4 + (tb % 2) * 2
            qb_ = sb_ + 1
            for dc in range(NDC):
                sl = dc % 2
                op("act", "act",
                   lambda e, d=qsb(sl), s=rbig[:, dc * 2048 + tb * 512:dc * 2048 + (tb + 1) * 512]:
                   e.activation(out=d, in_=s, func=AF.Square),
                   wbufs=[f"qs{sl}"], rbufs=[f"rb{dc}"], tag=f"sq{tb}.{dc}")
                mm(bank(sb_), onesm[:],
                   rbig[:, dc * 2048 + tb * 512:dc * 2048 + (tb + 1) * 512],
                   start=(dc == 0), stop=(dc == NDC - 1),
                   wbufs=([f"ps{sb_}"] if dc in (0, NDC - 1) else ()),
                   rbufs=["onesm", f"rb{dc}"], tag=f"sm{tb}.{dc}")
                mm(bank(qb_), onesm[:], qsb(sl),
                   start=(dc == 0), stop=(dc == NDC - 1),
                   wbufs=([f"ps{qb_}"] if dc in (0, NDC - 1) else ()),
                   rbufs=["onesm", f"qs{sl}"], tag=f"sv{tb}.{dc}")
            op("dve", "dve",
               lambda e, d=r2f[:, tb * 512:(tb + 1) * 512], s=bank(sb_):
               e.tensor_scalar_mul(out=d, in0=s, scalar1=1.0 / 2048.0),
               wbufs=[f"lnm{tb}"], rbufs=[f"ps{sb_}"], tag=f"m1.{tb}")
            op("dve", "dve",
               lambda e, d=r2f[:, 2048 + tb * 512:2048 + (tb + 1) * 512], s=bank(qb_):
               e.tensor_scalar_mul(out=d, in0=s, scalar1=1.0 / 2048.0),
               wbufs=[f"lnv{tb}"], rbufs=[f"ps{qb_}"], tag=f"v1.{tb}")
        lnm = [f"lnm{t}" for t in range(4)]
        lnv = [f"lnv{t}" for t in range(4)]
        op("dve", "dve",
           lambda e: e.tensor_tensor(out=stg[:, 0:2048], in0=r2f[:, 0:2048],
                                     in1=r2f[:, 0:2048], op=ALU.mult),
           wbufs=["xres"], rbufs=lnm, tag="mm1")
        op("dve", "dve",
           lambda e: e.tensor_sub(out=r2f[:, 2048:4096], in0=r2f[:, 2048:4096],
                                  in1=stg[:, 0:2048]),
           wbufs=lnv, rbufs=["xres"], waits=sum((B[b].rd_waits() for b in lnv), []),
           tag="var1")
        op("act", "act",
           lambda e: e.activation(out=r2f[:, 2048:4096], in_=r2f[:, 2048:4096],
                                  func=AF.Ln, bias=epsb[:]),
           wbufs=lnv, rbufs=["epsb"],
           waits=sum((B[b].wr_waits() for b in lnv), []), tag="ln1a")
        op("act", "act",
           lambda e: e.activation(out=r2f[:, 2048:4096], in_=r2f[:, 2048:4096],
                                  func=AF.Exp, scale=-0.5),
           wbufs=lnv, waits=sum((B[b].wr_waits() for b in lnv), []), tag="ln1b")
        op("dve", "dve",
           lambda e: e.scalar_tensor_tensor(out=stg[:, 0:2048], in0=r2f[:, 0:2048],
                                            scalar=-1.0, in1=r2f[:, 2048:4096],
                                            op0=ALU.mult, op1=ALU.mult),
           wbufs=["xres"], rbufs=lnm + lnv, tag="brow1")
        if dbg:
            dma("gp", "st", dbg_out["dr"], rbig, rbufs=[f"rb{d_}" for d_ in range(NDC)], tag="dbg_r")
        aslot = stg[:, 2048:3072].bitcast(BF16)[:, 0:2048]
        for dc in range(NDC):
            op("po", "po",
               lambda e, s=rbig[:, dc * 2048:(dc + 1) * 2048]:
               e.tensor_tensor(out=aslot, in0=s, in1=r2f[:, 2048:4096], op=ALU.mult),
               wbufs=["qs0", "qs1"], rbufs=[f"rb{dc}"] + lnv, tag=f"lA{dc}")
            op("po", "po",
               lambda e, d=xlnb[:, dc * 2048:(dc + 1) * 2048]:
               e.tensor_tensor(out=d, in0=aslot, in1=stg[:, 0:2048], op=ALU.add),
               wbufs=[f"xl{dc}"], rbufs=["qs0", "qs1", "xres"],
               waits=zall_read, tag=f"lB{dc}")

        upslot = lambda s: wsb[:, s * 2048:(s + 1) * 2048]
        dnslot = lambda s: wsb[:, (2 + s) * 2048:(3 + s) * 2048]
        def ln2_stat(tg, dc):
            sl = dc % 2
            op("act", "act",
               lambda e, d=qsb(sl), s=r2b[:, dc * 512:(dc + 1) * 512]:
               e.activation(out=d, in_=s, func=AF.Square),
               wbufs=[f"qs{sl}"], rbufs=[f"r2{dc}"], tag=f"s2{tg}.{dc}")
            mm(bank(6), onesm[:], r2b[:, dc * 512:(dc + 1) * 512],
               start=(dc == 0), stop=(dc == NDC - 1),
               wbufs=(["ps6"] if dc in (0, NDC - 1) else ()), rbufs=["onesm", f"r2{dc}"],
               tag=f"sm2.{tg}.{dc}")
            mm(bank(7), onesm[:], qsb(sl),
               start=(dc == 0), stop=(dc == NDC - 1),
               wbufs=(["ps7"] if dc in (0, NDC - 1) else ()), rbufs=["onesm", f"qs{sl}"],
               tag=f"sv2.{tg}.{dc}")

        for tg in range(4):
            for c in range(NDC):
                mm(bank(3)[0:R, :], aup_sb[:, c * R:(c + 1) * R],
                   xlnb[:, c * 2048 + tg * 512:c * 2048 + (tg + 1) * 512],
                   start=(c == 0), stop=(c == NDC - 1),
                   wbufs=(["ps3"] if c in (0, NDC - 1) else ()), rbufs=["aup", f"xl{c}"],
                   tag=f"tu{tg}.{c}")
            op("dve", "dve", lambda e: e.tensor_copy(out=tup_sb[:], in_=bank(3)[0:R, :]),
               wbufs=["tupb"], rbufs=["ps3"], tag=f"tue{tg}")
            for hc in range(NHC):
                s = hc % 2
                dma("sp", f"wsl{s}", upslot(s).rearrange("p (k e) -> p k e", k=NDC),
                    gup[hc], wbuf=f"wsb{s}",
                    waits=(grids_done if (tg == 0 and hc < 2) else ()),
                    tag=f"gu{tg}.{hc}")
                if hc % 8 == 0:
                    bl = (hc // 8) % 2
                    dma("sp", f"bl{bl}",
                        bupl_sb[:, bl * 1024:(bl + 1) * 1024],
                        bupl_scr[:, hc * 128:(hc + 8) * 128],
                        wbuf=f"bupl{bl}", rbufs=["buplscr"], tag=f"bl{tg}.{hc}")
                bi = hc % 2
                dst = bank(bi)
                for c in range(NDC):
                    rb = [f"wsb{s}"] if c in (0, NDC - 1) else []
                    mm(dst, upslot(s)[:, c * 128:(c + 1) * 128],
                       xlnb[:, c * 2048 + tg * 512:c * 2048 + (tg + 1) * 512],
                       start=(c == 0), stop=False,
                       wbufs=([f"ps{bi}"] if c == 0 else ()), rbufs=rb,
                       tag=f"u{tg}.{hc}.{c}")
                bl = (hc // 8) % 2
                mm(dst, bupl_sb[:, bl * 1024 + (hc % 8) * 128:bl * 1024 + (hc % 8 + 1) * 128],
                   tup_sb[:], start=False, stop=True, wbufs=[f"ps{bi}"],
                   rbufs=[f"bupl{bl}", "tupb"], tag=f"ul{tg}.{hc}")
                wx = [f"x2{hc}"] + ([f"rb{hc // 4}"] if tg == 0 else [])
                op("act", "act",
                   lambda e, d=x2b[:, hc * 512:(hc + 1) * 512], s=dst, b=bup_sb[:, hc:hc + 1]:
                   e.activation(out=d, in_=s, func=AF.Relu, bias=b, scale=1.0),
                   wbufs=wx, rbufs=[f"ps{bi}", "bup"], tag=f"re{tg}.{hc}")
                if hc >= 1:
                    h2 = hc - 1
                    mm(bank(3)[0:R, :], adn_sb[:, h2 * R:(h2 + 1) * R],
                       x2b[:, h2 * 512:(h2 + 1) * 512],
                       start=(h2 == 0), stop=False,
                       wbufs=(["ps3"] if h2 == 0 else ()), rbufs=["adn", f"x2{h2}"],
                       tag=f"t2{tg}.{h2}")
                if hc == NHC - 1:
                    mm(bank(3)[0:R, :], adn_sb[:, hc * R:(hc + 1) * R],
                       x2b[:, hc * 512:(hc + 1) * 512],
                       start=False, stop=True,
                       wbufs=["ps3"], rbufs=["adn", f"x2{hc}"],
                       tag=f"t2{tg}.{hc}")
            if dbg and tg == 0:
                dma("gp", "st", dbg_out["dxl"], xlnb,
                    rbufs=[f"xl{d_}" for d_ in range(NDC)], tag="dbg_xl")
                dma("gp", "st", dbg_out["dx2"], x2b,
                    rbufs=[f"x2{h_}" for h_ in range(NHC)], tag="dbg_x2")
            op("dve", "dve", lambda e: e.tensor_copy(out=t2_sb[:], in_=bank(3)[0:R, :]),
               wbufs=["t2b"], rbufs=["ps3"], tag=f"t2e{tg}")
            for dc in range(NDC):
                bi = 4 + (dc % 2)
                dst = bank(bi)
                for qt in range(4):
                    s = qt % 2
                    dma("sp", f"wsl{2 + s}", dnslot(s).rearrange("p (k e) -> p k e", k=16),
                        gdn[dc, qt],
                        wbuf=f"wsb{2 + s}",
                        waits=(grids_done if (tg == 0 and dc == 0 and qt < 2) else ()),
                        tag=f"gd{tg}.{dc}.{qt}")
                    for kk in range(16):
                        hc = qt * 16 + kk
                        rb = [f"x2{hc}"] if dc in (0, NDC - 1) else []
                        if kk in (0, 15):
                            rb.append(f"wsb{2 + s}")
                        mm(dst, dnslot(s)[:, kk * 128:(kk + 1) * 128],
                           x2b[:, hc * 512:(hc + 1) * 512],
                           start=(hc == 0), stop=False,
                           wbufs=([f"ps{bi}"] if hc == 0 else ()), rbufs=rb,
                           tag=f"d{tg}.{dc}.{hc}")
                mm(dst, bdnl_sb[:, dc * 128:(dc + 1) * 128], t2_sb[:],
                   start=False, stop=True, wbufs=[f"ps{bi}"],
                   rbufs=["bdnl", "t2b"], tag=f"dl{tg}.{dc}")
                op("dve", "dve",
                   lambda e, d=r2b[:, dc * 512:(dc + 1) * 512], s=dst,
                          b=bdn_sb[:, dc:dc + 1],
                          x=xlnb[:, dc * 2048 + tg * 512:dc * 2048 + (tg + 1) * 512]:
                   e.scalar_tensor_tensor(out=d, in0=s, scalar=b, in1=x,
                                          op0=ALU.add, op1=ALU.add),
                   wbufs=[f"r2{dc}"], rbufs=[f"ps{bi}", "bdn", f"xl{dc}"],
                   tag=f"r2.{tg}.{dc}")
                if dc >= 1:
                    ln2_stat(tg, dc - 1)
            ln2_stat(tg, NDC - 1)
            op("dve", "dve",
               lambda e: e.tensor_scalar_mul(out=stg[:, 0:512], in0=bank(6),
                                             scalar1=1.0 / 2048.0),
               wbufs=["l2m"], rbufs=["ps6"], tag=f"m2.{tg}")
            op("dve", "dve",
               lambda e: e.tensor_scalar_mul(out=stg[:, 512:1024], in0=bank(7),
                                             scalar1=1.0 / 2048.0),
               wbufs=["l2v"], rbufs=["ps7"], tag=f"v2.{tg}")
            op("dve", "dve",
               lambda e: e.tensor_tensor(out=stg[:, 1024:1536], in0=stg[:, 0:512],
                                         in1=stg[:, 0:512], op=ALU.mult),
               wbufs=["l2t"], rbufs=["l2m"], tag=f"mm2.{tg}")
            op("dve", "dve",
               lambda e: e.tensor_sub(out=stg[:, 512:1024], in0=stg[:, 512:1024],
                                      in1=stg[:, 1024:1536]),
               wbufs=["l2v"], rbufs=["l2t"], waits=B["l2v"].wr_waits(), tag=f"va2.{tg}")
            op("act", "act",
               lambda e: e.activation(out=stg[:, 512:1024], in_=stg[:, 512:1024],
                                      func=AF.Ln, bias=epsb[:]),
               wbufs=["l2v"], rbufs=["epsb"], waits=B["l2v"].wr_waits(), tag=f"lna2.{tg}")
            op("act", "act",
               lambda e: e.activation(out=stg[:, 512:1024], in_=stg[:, 512:1024],
                                      func=AF.Exp, scale=-0.5),
               wbufs=["l2v"], waits=B["l2v"].wr_waits(), tag=f"lnb2.{tg}")
            op("dve", "dve",
               lambda e: e.scalar_tensor_tensor(out=stg[:, 1024:1536], in0=stg[:, 0:512],
                                                scalar=-1.0, in1=stg[:, 512:1024],
                                                op0=ALU.mult, op1=ALU.mult),
               wbufs=["l2t"], rbufs=["l2m", "l2v"], tag=f"br2.{tg}")
            for dc in range(NDC):
                sl = dc % 2
                op("po", "po",
                   lambda e, d=qsf(sl), s=r2b[:, dc * 512:(dc + 1) * 512]:
                   e.tensor_tensor(out=d, in0=s, in1=stg[:, 512:1024], op=ALU.mult),
                   wbufs=[f"qs{sl}"], rbufs=[f"r2{dc}", "l2v"], tag=f"fA{tg}.{dc}")
                op("po", "po",
                   lambda e, d=qsf(sl):
                   e.tensor_add(out=d, in0=d, in1=stg[:, 1024:1536]),
                   wbufs=[f"qs{sl}"], rbufs=["l2t"], waits=B[f"qs{sl}"].wr_waits(),
                   tag=f"fB{tg}.{dc}")
                dma("gp", "st", out[dc * 128:(dc + 1) * 128, tg * 512:(tg + 1) * 512],
                    qsf(sl), rbufs=[f"qs{sl}"], tag=f"w{tg}.{dc}")

        st_total = p.cnt["st"]
        p.emit("sp", lambda e: e.nop(), [("st", st_total)], [], "fin_sp")
        p.emit("gp", lambda e: e.nop(), [("st", st_total)], [], "fin_gp")

        p.check()

        nc.all_engine_barrier()
        for s_ in sems.values():
            nc.gpsimd.sem_clear(s_)
        nc.all_engine_barrier()

        with nc.Block() as block:
            def runner(key):
                def body(eng):
                    for e_, fn, ws, incs, _tag in p.ops:
                        if e_ != key:
                            continue
                        for s, v in ws:
                            eng.wait_ge(sems[s], v)
                        inst = fn(eng)
                        for s, n in incs:
                            inst.then_inc(sems[s], n)
                return body
            block.tensor(runner("pe"))
            block.scalar(runner("act"))
            block.vector(runner("dve"))
            block.gpsimd(runner("gp"))
            block.sync(runner("sp"))

    return nc


_NC_CACHE = {}


def _get_nc():
    if "nc" not in _NC_CACHE:
        _NC_CACHE["nc"] = build_program()
    return _NC_CACHE["nc"]


def kernel(L0_attn_x, L0_attn_W_Q, L0_attn_W_K, L0_attn_W_V, L0_attn_W_O,
           L0_attn_b_Q, L0_attn_b_K, L0_attn_b_V, L0_attn_b_O,
           L0_ffn_w_up_q, L0_ffn_w_up_scale, L0_ffn_b_up,
           L0_ffn_w_up_lora_a, L0_ffn_w_up_lora_b,
           L0_ffn_w_down_q, L0_ffn_w_down_scale, L0_ffn_b_down,
           L0_ffn_w_down_lora_a, L0_ffn_w_down_lora_b):
    x = np.asarray(L0_attn_x, dtype=np.float32)
    nb = x.shape[0]
    f32 = lambda a: np.ascontiguousarray(np.asarray(a, dtype=np.float32))

    wq_l = f32(np.asarray(L0_attn_W_Q).transpose(1, 0, 2).reshape(D, H * E))
    wk_l = f32(np.asarray(L0_attn_W_K).transpose(1, 0, 2).reshape(D, H * E))
    wv_l = f32(np.asarray(L0_attn_W_V).transpose(1, 0, 2).reshape(D, H * E))
    wo_l = f32(np.asarray(L0_attn_W_O).reshape(H * E, D))
    bq_l = f32(np.asarray(L0_attn_b_Q).T)
    bk_l = f32(np.asarray(L0_attn_b_K).T)
    bv_l = f32(np.asarray(L0_attn_b_V).reshape(1, H * E))
    bo_l = f32(np.asarray(L0_attn_b_O).reshape(NDC, 128).T)
    qup_l = np.ascontiguousarray(np.asarray(L0_ffn_w_up_q, dtype=np.int32).T)
    sup_l = f32(np.asarray(L0_ffn_w_up_scale).T)
    qdn_l = np.ascontiguousarray(np.asarray(L0_ffn_w_down_q, dtype=np.int32).T)
    sdn_l = f32(np.asarray(L0_ffn_w_down_scale).T)
    sel_l = np.zeros((2, 128), np.float32)
    sel_l[0, 0:64] = 1.0
    sel_l[1, 64:128] = 1.0
    bup_l = f32(np.asarray(L0_ffn_b_up).reshape(NHC, 128).T)
    bdn_l = f32(np.asarray(L0_ffn_b_down).reshape(NDC, 128).T)
    aup_l = f32(L0_ffn_w_up_lora_a)
    bupl_l = f32(L0_ffn_w_up_lora_b)
    adn_l = f32(L0_ffn_w_down_lora_a)
    bdnl_l = f32(L0_ffn_w_down_lora_b)

    NEG = np.float32(-1e9)
    triu = (np.arange(128)[:, None] > np.arange(128)[None, :]).astype(np.float32) * NEG
    allm = np.full((128, 128), NEG, np.float32)
    zero = np.zeros((128, 128), np.float32)
    msk_p = [np.concatenate([triu, allm], axis=1),
             np.concatenate([zero, triu], axis=1)]
    idn_l = np.eye(128, dtype=np.float32)

    in_maps = []
    for c in range(8):
        b = c // 2
        par = c % 2
        xb = x[b]
        xfm_c = np.ascontiguousarray(xb.T)
        xown_c = np.ascontiguousarray(
            xb.reshape(32, 128, D)[par::2].reshape(TOWN, D).T)
        in_maps.append(dict(
            xfm=xfm_c, xown=xown_c, wq=wq_l, wk=wk_l, wv=wv_l, wo=wo_l,
            bq=bq_l, bk=bk_l, bv=bv_l, bo=bo_l, msk=np.ascontiguousarray(msk_p[par]),
            qupt=qup_l, supt=sup_l, qdnt=qdn_l, sdnt=sdn_l, sel2=sel_l, idn=idn_l,
            bup=bup_l, bdn=bdn_l, aupw=aup_l, buplw=bupl_l, adnw=adn_l,
            bdnlw=bdnl_l))

    nc = _get_nc()
    res = run_bass_kernel_spmd(nc, in_maps, core_ids=list(range(8)))

    y = np.empty((nb, T, D), dtype=np.float32)
    for c in range(8):
        b = c // 2
        par = c % 2
        o = res.results[c]["out"]
        y[b].reshape(32, 128, D)[par::2] = o.T.reshape(16, 128, D)
    return y
